# version 12
# baseline (speedup 1.0000x reference)
import numpy as np
import ml_dtypes
import concourse.bass as bass
import concourse.mybir as mybir
from concourse.bass_utils import run_bass_kernel_spmd
from contextlib import ExitStack

F32 = mybir.dt.float32
BF16 = mybir.dt.bfloat16
AF = mybir.ActivationFunctionType
ALU = mybir.AluOpType
AX = mybir.AxisListType
BF = ml_dtypes.bfloat16

DM = 1024
NH = 16
HKV = 4
DH = 64
E = 8
CH = 512
EPS = 1e-5
PERM = [0, 4, 1, 5, 2, 6, 3, 7, 8, 12, 9, 13, 10, 14, 11, 15]

SEMS = ["dc", "dw", "dx", "dq", "dqr", "dr", "do", "t", "v", "s", "g"]


def rope_tables(S):
    inv = 1.0 / (10000.0 ** (np.arange(0, 64, 2, dtype=np.float64) / 64.0))
    fr = np.arange(S, dtype=np.float64)[:, None] * inv[None, :]
    emb = np.concatenate([fr, fr], axis=1)
    cos = np.cos(emb)[:, 0::2]
    sin = np.sin(emb)[:, 0::2]
    C = np.empty((128, S), np.float64)
    Sg = np.empty((128, S), np.float64)
    for r in range(128):
        dl = r % 64
        pair = dl // 2
        C[r] = cos[:, pair]
        Sg[r] = sin[:, pair] * (-1.0 if dl % 2 == 0 else 1.0)
    return C.astype(np.float32), Sg.astype(np.float32)


class Em:

    def __init__(self, nc, sems, name=None, eng=None, serialize=False):
        self.nc = nc
        self.sems = sems
        self.cnt = {k: 0 for k in sems}
        self.name = name
        self.eng = eng
        self.lastw = {}
        self.mark = {}
        self.serialize = serialize

    def setmark(self, key, sem):
        self.mark[key] = (sem, self.cnt[sem])

    def m(self, key):
        return self.mark.get(key, ("v", 0))

    def op(self, engine, build, inc=None, waits=(), lp=False):
        if self.name == engine:
            if self.serialize:
                waits = list(waits) + [(s, self.cnt[s]) for s in self.sems]
            for (s, v) in waits:
                if v <= 0:
                    continue
                if v > self.lastw.get(s, 0):
                    self.eng.wait_ge(self.sems[s], v)
                    self.lastw[s] = v
            if lp:
                with self.nc.allow_low_precision("bf16 matmul operand prep"):
                    ins = build(self.eng)
            else:
                ins = build(self.eng)
            if inc is not None and ins is not None:
                ins.then_inc(self.sems[inc[0]], inc[1])
        if inc is not None:
            self.cnt[inc[0]] += inc[1]


def build_nc(S, dbg=False, serialize=False):
    NCH = S // CH
    nc = bass.Bass("TRN2", target_bir_lowering=False)

    dr = {}
    def di(name, shape, dt):
        dr[name] = nc.dram_tensor(name, list(shape), dt, kind="ExternalInput").ap()
    di("x", (S, DM), F32)
    di("wq", (DM, DM), F32)
    di("wk", (DM, 256), F32)
    di("wv", (DM, 256), F32)
    di("wo", (DM, DM), F32)
    di("wg", (DM, E), F32)
    di("bg", (1, E), F32)
    di("w1", (E, DM, DM), F32)
    di("b1r", (128, 64), F32)
    di("w2", (E, DM, DM), F32)
    di("b2", (E, DM), F32)
    di("ropec", (128, S), F32)
    di("ropes", (128, S), F32)
    di("idbf", (128, 128), BF16)
    di("idf", (128, 128), F32)
    di("trim", (128, 128), BF16)
    di("pswap", (128, 128), BF16)
    out = nc.dram_tensor("out", [S, DM], F32, kind="ExternalOutput").ap()
    qspill = nc.dram_tensor("qspill", [NCH, 128, 8 * CH], BF16, kind="Internal").ap()
    dbgt = {}
    if dbg:
        for nm, shape, dt in [
            ("d_xt", (128, 8, CH), BF16), ("d_qst", (128, 8, CH), BF16),
            ("d_kt", (128, 2, S), BF16), ("d_vb", (128, S // 128, 4, 66), BF16),
            ("d_km", (1, 4), F32), ("d_qm", (1, NCH), F32),
            ("d_qtc", (128, 8, CH), BF16), ("d_attnt", (128, 8, CH), BF16),
            ("d_x1", (128, 4, DM), F32), ("d_x1t", (128, 8, CH), BF16),
            ("d_w", (128, 4, E), F32), ("d_yacc", (128, 8, CH), F32),
            ("d_bias", (128, 1), F32), ("d_pt", (128, 4, CH), BF16),
        ]:
            dbgt[nm] = nc.dram_tensor(nm, list(shape), dt, kind="ExternalOutput").ap()

    B = {}
    def sba(name, shape, dt):
        B[name] = nc.alloc_sbuf_tensor(name, list(shape), dt)
        return B[name]

    NT = S // 128
    sba("KT", (128, 2, S), BF16)
    sba("Vb", (128, NT, 4, 66), BF16)
    sba("Wos", (128, 8, DM), BF16)
    sba("Wgs", (128, 8, E), BF16)
    sba("b1r_s", (128, 64), F32)
    sba("b2_s", (E, DM), BF16)
    sba("bg_s", (1, E), BF16)
    sba("idbf_s", (128, 128), BF16)
    sba("idf_s", (128, 128), F32)
    sba("trim_s", (128, 128), BF16)
    sba("pswap_s", (128, 128), BF16)
    sba("onesb", (1, 128), BF16)
    sba("onescol", (128, 1), BF16)
    sba("kmax2", (1, 4), F32)
    sba("qmax2", (1, NCH), F32)
    sba("bias_sb", (128, 1), F32)
    sba("cscr", (1, 4), F32)
    sba("cscrb", (1, 1), BF16)
    base0 = nc.sbuf_base
    sba("Wqs", (128, 8, DM), BF16)
    sba("Wks", (128, 8, 256), BF16)
    sba("Wvs", (128, 8, 256), BF16)
    sba("ropecs", (128, 2, CH), F32)
    sba("Xb", (128, 2, 4, DM), BF16)
    sba("XTc", (128, 2, 8, CH), BF16)
    sba("QTstage", (128, 8, CH), BF16)
    sba("Qev", (128, 2, CH), BF16)
    sba("Qsh", (128, 2, CH), BF16)
    sba("scr1", (128, 2, CH), F32)
    sba("scr2", (128, 2, CH), F32)
    sba("sqscr", (128, 2, CH), BF16)
    sba("qns", (1, CH), F32)
    ph1_top = nc.sbuf_base
    nc.sbuf_base = base0
    sba("XM", (128, 4, DM), F32)
    sba("z2", (128, 2, DM), F32)
    sba("bst", (128, DM), BF16)
    sba("X1T", (128, 8, CH), BF16)
    sba("stats", (128, 16), F32)
    sba("gsc", (128, 32), F32)
    sba("wch", (128, 4, E), F32)
    sba("wTb", (E, 4, 128), BF16)
    sba("wrow", (1, 2, CH), BF16)
    sba("wb_sb", (128, CH), BF16)
    sba("Hb", (128, 2, CH), BF16)
    sba("Hw", (128, 8, CH), BF16)
    sba("Yacc", (128, 8, CH), F32)
    sba("Wsl", (128, 3, 8, DM), BF16)
    nc.sbuf_base = max(nc.sbuf_base, ph1_top)
    sba("QTc", (128, 2, 8, CH), BF16)
    sba("attnT", (128, 2, 8, CH), BF16)
    sba("attnraw", (128, CH), BF16)
    sba("rsb", (1, 2 * CH), BF16)
    sba("lnr", (1, CH), F32)
    sba("PT", (128, 4, CH), BF16)

    ctx = ExitStack()
    e_ = ctx.enter_context
    def ps(name, shape, dt):
        B[name] = e_(nc.psum_tensor(name, list(shape), dt))
        return B[name]
    ps("pA", (128, 512), F32)
    ps("pB", (128, 512), F32)
    ps("pC", (128, 512), F32)
    ps("pD", (128, 512), F32)
    ps("pE", (128, 512), F32)
    ps("pF", (128, 512), F32)
    ps("pG", (128, 512), F32)
    ps("pH", (128, 1024), BF16)

    sems = {k: e_(nc.semaphore(k)) for k in SEMS}

    def sched(em):
        KT, Vb, Wos, Wgs = (B[k] for k in ("KT", "Vb", "Wos", "Wgs"))
        pA, pB, pC, pD, pE, pF, pG, pH = (B[k] for k in
            ("pA", "pB", "pC", "pD", "pE", "pF", "pG", "pH"))

        for nm in ("idbf", "idf", "trim", "pswap", "b1r"):
            dst = B[nm + "_s"]
            src = dr[nm]
            em.op("sync", lambda e, d=dst, s=src: e.dma_start(out=d[:], in_=s),
                  inc=("dc", 16))
        em.op("gpsimd", lambda e: e.dma_start(
            out=B["Wqs"][:], in_=dr["wq"].rearrange("(j p) d -> p j d", p=128)), inc=("dw", 16))
        em.op("gpsimd", lambda e: e.dma_start(
            out=B["Wks"][:], in_=dr["wk"].rearrange("(j p) d -> p j d", p=128)), inc=("dw", 16))
        em.op("gpsimd", lambda e: e.dma_start(
            out=B["Wvs"][:], in_=dr["wv"].rearrange("(j p) d -> p j d", p=128)), inc=("dw", 16))
        em.op("gpsimd", lambda e: e.dma_start(
            out=Wos[:], in_=dr["wo"].rearrange("(j p) d -> p j d", p=128)), inc=("dw", 16))
        em.op("gpsimd", lambda e: e.dma_start(
            out=Wgs[:], in_=dr["wg"].rearrange("(j p) d -> p j d", p=128)), inc=("dw", 16))
        em.op("gpsimd", lambda e: e.dma_start(out=B["b2_s"][:], in_=dr["b2"]), inc=("dw", 16))
        em.op("gpsimd", lambda e: e.dma_start(out=B["bg_s"][:], in_=dr["bg"]), inc=("dw", 16))
        em.setmark("w_attn", "dw")
        em.op("vector", lambda e: e.memset(B["onesb"][:], 1.0), inc=("v", 1))
        em.op("vector", lambda e: e.memset(B["onescol"][:], 1.0), inc=("v", 1))
        em.op("vector", lambda e: e.memset(
            Vb[:].rearrange("p a b c -> p (a b c)"), 1.0), inc=("v", 1))
        em.op("vector", lambda e: e.memset(B["kmax2"][:], 0.0), inc=("v", 1))

        def ph1_steps(c):
            steps = []
            def step(f, w=1.0):
                steps.append((f, w))
            cc = slice(c * CH, (c + 1) * CH)
            par = c % 2
            step(lambda: em.op("gpsimd", lambda e, c=c, par=par: e.dma_start(
                out=B["Xb"][:, par, :, :],
                in_=dr["x"][c * CH:(c + 1) * CH, :].rearrange("(ts p) d -> p ts d", p=128)),
                waits=[em.m(f"xbfree{par}")], inc=("dx", 16)), 0.3)
            def mk_tr(ts, nld):
              def f():
                def tr(e, par=par, ts=ts):
                    for j in range(8):
                        ins = e.transpose(pH[:, j * 128:(j + 1) * 128],
                                          B["Xb"][:, par, ts, j * 128:(j + 1) * 128],
                                          B["idbf_s"][:])
                    return ins
                em.op("tensor", tr,
                      waits=[("dx", 16 * nld), ("dc", 16), em.m("pHfree"),
                             em.m(f"xt_free{par}")],
                      inc=("t", 1))
                em.setmark("pH_t", "t")
                if ts == 3:
                    em.setmark(f"xbfree{par}", "t")
                em.op("scalar", lambda e, par=par, ts=ts: e.copy(
                    B["XTc"][:, par, 0:8, ts * 128:(ts + 1) * 128],
                    pH[:].rearrange("p (j q) -> p j q", j=8)),
                    waits=[em.m("pH_t")], inc=("s", 1))
                em.setmark("pHfree", "s")
                if ts == 3:
                    em.setmark("xtc_ready", "s")
              return f
            nld = em.ph1_nld = getattr(em, "ph1_nld", 0) + 1
            for ts in range(4):
                step(mk_tr(ts, nld), 1.2)
            def ropeld():
                em.op("sync", lambda e, c=c: e.dma_start(
                    out=B["ropecs"][:, 0, :], in_=dr["ropec"][:, c * CH:(c + 1) * CH]),
                    waits=[em.m("ropefree")], inc=("dc", 16))
                em.op("sync", lambda e, c=c: e.dma_start(
                    out=B["ropecs"][:, 1, :], in_=dr["ropes"][:, c * CH:(c + 1) * CH]),
                    inc=("dc", 16))
                em.setmark("rope_ld", "dc")
            step(ropeld, 0.2)

            projs = ([("v", ts) for ts in range(4)] + [("k", 0), ("k", 1)]
                     + [("q", jq) for jq in range(8)])
            def dst_ap(kind, idx, c=c):
                if kind == "k":
                    return B["KT"][:, idx, c * CH:(c + 1) * CH]
                return B["QTstage"][:, idx, :]

            def stage0(j):
                kind, idx = projs[j]
                pp = j % 2
                pX = pF if pp == 0 else pG
                fm = "pFfree" if pp == 0 else "pGfree"
                tm = "pF_t" if pp == 0 else "pG_t"
                if kind == "v":
                    def vgrp(e, par=par, idx=idx, pX=pX):
                        for jj in range(8):
                            ins = e.matmul(pX[:, 0:256],
                                           B["XTc"][:, par, jj, idx * 128:(idx + 1) * 128],
                                           B["Wvs"][:, jj, :], start=(jj == 0), stop=(jj == 7))
                        return ins
                    em.op("tensor", vgrp,
                          waits=[em.m("xtc_ready"), em.m("w_attn"), em.m(fm)],
                          inc=("t", 1))
                    em.setmark(tm, "t")
                    em.op("scalar", lambda e, c=c, idx=idx, pX=pX: e.copy(
                        Vb[:, c * 4 + idx, :, 0:64],
                        pX[:, 0:256].rearrange("p (h d) -> p h d", h=4)),
                        waits=[em.m(tm)], inc=("s", 1))
                    em.setmark(fm, "s")
                    return
                lhsrc = B["Wks"] if kind == "k" else B["Wqs"]
                def grp(e, par=par, kind=kind, idx=idx, pX=pX):
                    for jj in range(8):
                        ins = e.matmul(pX[:, :], lhsrc[:, jj, idx * 128:(idx + 1) * 128],
                                       B["XTc"][:, par, jj, :],
                                       start=(jj == 0), stop=(jj == 7))
                    return ins
                em.op("tensor", grp,
                      waits=[em.m("xtc_ready"), em.m("w_attn"), em.m(fm)],
                      inc=("t", 1))
                em.setmark(tm, "t")
                em.op("scalar", lambda e, pp=pp, pX=pX: e.copy(B["Qev"][:, pp, :], pX[:]),
                      waits=[em.m(tm), ("dq", 16 * c), em.m(f"qevrd{pp}")],
                      inc=("s", 1))
                em.setmark(f"qev_s{pp}", "s")
                em.op("tensor", lambda e, pp=pp, pX=pX: e.matmul(
                    pX[:], B["pswap_s"][:], B["Qev"][:, pp, :], start=True, stop=True),
                    waits=[em.m(f"qev_s{pp}"), ("dc", 16 * 4)], inc=("t", 1))
                em.setmark(tm, "t")
                em.op("scalar", lambda e, pp=pp, pX=pX: e.copy(B["Qsh"][:, pp, :], pX[:]),
                      waits=[em.m(tm), em.m(f"qshrd{pp}")], inc=("s", 1))
                em.setmark(fm, "s")
                em.setmark(f"qsh_s{pp}", "s")

            def stage1(j):
                kind, idx = projs[j]
                if kind == "v":
                    return
                pp = j % 2
                em.op("vector", lambda e, pp=pp: e.tensor_tensor(
                    B["scr1"][:, pp, :], B["Qev"][:, pp, :], B["ropecs"][:, 0, :],
                    ALU.mult),
                    waits=[em.m("rope_ld"), em.m(f"qev_s{pp}")], inc=("v", 1))
                em.setmark(f"qevrd{pp}", "v")
                em.op("vector", lambda e, pp=pp: e.tensor_tensor(
                    B["scr2"][:, pp, :], B["Qsh"][:, pp, :], B["ropecs"][:, 1, :],
                    ALU.mult),
                    waits=[em.m(f"qsh_s{pp}")], inc=("v", 1))
                em.setmark(f"qshrd{pp}", "v")
                em.op("vector", lambda e, kind=kind, idx=idx, pp=pp: e.tensor_tensor(
                    dst_ap(kind, idx), B["scr1"][:, pp, :], B["scr2"][:, pp, :],
                    ALU.add), inc=("v", 1), lp=True)
                em.op("vector", lambda e, kind=kind, idx=idx, pp=pp: e.tensor_tensor(
                    B["sqscr"][:, pp, :], dst_ap(kind, idx), dst_ap(kind, idx),
                    ALU.mult),
                    waits=[em.m(f"sq_t{pp}")], inc=("v", 1), lp=True)
                em.setmark(f"rot_v{pp}", "v")

            def stage2(j):
                kind, idx = projs[j]
                if kind == "v":
                    return
                pp = j % 2
                em.op("tensor", lambda e, pp=pp: e.matmul(
                    pE[0:1, :], B["onescol"][:], B["sqscr"][:, pp, :],
                    start=True, stop=True, skip_group_check=True),
                    waits=[em.m(f"rot_v{pp}"), em.m("pEfree")], inc=("t", 1))
                em.setmark("pE_t", "t")
                em.setmark(f"sq_t{pp}", "t")
                if idx == 0:
                    em.op("vector", lambda e: e.tensor_copy(
                        B["qns"][0:1, :], pE[0:1, :]),
                        waits=[em.m("pE_t")], inc=("v", 1))
                else:
                    em.op("vector", lambda e: e.tensor_tensor(
                        B["qns"][0:1, :], B["qns"][0:1, :], pE[0:1, :], ALU.add),
                        waits=[em.m("pE_t")], inc=("v", 1))
                em.setmark("pEfree", "v")
                if kind == "k" and idx == 1:
                    em.op("vector", lambda e: e.reduce_max(
                        B["kmax2"][0:1, 1:2], B["qns"][0:1, :], axis=AX.X),
                        waits=[("v", em.cnt["v"])], inc=("v", 1))
                    em.op("vector", lambda e: e.tensor_tensor(
                        B["kmax2"][0:1, 0:1], B["kmax2"][0:1, 0:1],
                        B["kmax2"][0:1, 1:2], ALU.max),
                        waits=[("v", em.cnt["v"])], inc=("v", 1))
                if kind == "q" and idx == 7:
                    em.op("vector", lambda e, c=c: e.reduce_max(
                        B["qmax2"][0:1, c:c + 1], B["qns"][0:1, :], axis=AX.X),
                        waits=[("v", em.cnt["v"])], inc=("v", 1))

            def mk_pipe(j):
                def f():
                    if j < len(projs):
                        stage0(j)
                    if 1 <= j <= len(projs):
                        stage1(j - 1)
                    if j >= 2:
                        stage2(j - 2)
                return f
            for j in range(len(projs) + 2):
                step(mk_pipe(j), 1.6)
            def fin():
                em.setmark("ropefree", "v")
                em.setmark("qt_v", "v")
                em.op("sync", lambda e, c=c: e.dma_start(
                    out=qspill[c], in_=B["QTstage"][:].rearrange("p a b -> p (a b)")),
                    waits=[em.m("qt_v")], inc=("dq", 16))
                em.setmark(f"xt_free{par}", "t")
            step(fin, 0.2)
            return steps

        NCH_ = S // CH

        def dump(nm, buf):
            if not dbg:
                return
            em.op("sync", lambda e, nm=nm, buf=buf: e.dma_start(
                out=dbgt[nm], in_=buf),
                waits=[("t", em.cnt["t"]), ("v", em.cnt["v"]), ("s", em.cnt["s"]),
                       ("dw", em.cnt["dw"]), ("dx", em.cnt["dx"]),
                       ("dqr", em.cnt["dqr"]), ("dr", em.cnt["dr"])],
                inc=("dc", 16))

        ph1 = {}
        ph1c0 = {}

        st = {"nout": 0, "wseq": 0, "nrow": 0, "z2war": [0, 0]}

        def attn_prologue(c, phj=None):
            joinw = ([("t", phj["t"]), ("v", phj["v"]), ("s", phj["s"])]
                     if phj is not None else [])
            em.op("sync", lambda e, c=c: e.dma_start(
                out=B["QTc"][:, c % 2, :, :].rearrange("p a b -> p (a b)"),
                in_=qspill[c]),
                waits=[("dq", 16 * (c + 1)), em.m(f"qtc_free{c % 2}")] + joinw,
                inc=("dqr", 16))
            em.op("vector", lambda e, c=c: e.tensor_tensor(
                B["cscr"][0:1, 0:1], B["qmax2"][0:1, c:c + 1], B["kmax2"][0:1, 0:1],
                ALU.mult), waits=[em.m("bias_used_s"), ("v", em.cnt["v"])],
                inc=("v", 1))
            em.setmark("cb_v", "v")
            em.op("scalar", lambda e: e.activation(
                B["cscr"][0:1, 1:2], B["cscr"][0:1, 0:1], AF.Ln),
                waits=[em.m("cb_v")], inc=("s", 1))
            em.op("scalar", lambda e: e.activation(
                B["cscr"][0:1, 2:3], B["cscr"][0:1, 1:2], AF.Exp, scale=0.5),
                waits=[("s", em.cnt["s"])], inc=("s", 1))
            em.setmark("cb_s", "s")
            em.op("vector", lambda e: e.tensor_copy(B["cscrb"][0:1, 0:1],
                                                    B["cscr"][0:1, 2:3]),
                  waits=[em.m("cb_s")], inc=("v", 1))
            em.setmark("cbb_v", "v")
            em.op("tensor", lambda e: e.matmul(
                pE[:, 0:1], B["onesb"][:], B["cscrb"][0:1, 0:1], start=True, stop=True,
                skip_group_check=True),
                waits=[em.m("cbb_v"), em.m("pEfree")] + joinw, inc=("t", 1))
            em.setmark("pE_t", "t")
            em.op("vector", lambda e: e.tensor_scalar(
                B["bias_sb"][:], pE[:, 0:1], -0.125, None, ALU.mult),
                waits=[em.m("pE_t")], inc=("v", 1))
            em.setmark("pEfree", "v")
            em.setmark("bias_v", "v")

        def attn_steps(c, phj=None):
            nkv = 4 * (c + 1)
            steps = []

            def st_exp(pr, i, kt):
                qoff = 0 if i < 4 * c else (i - 4 * c) * 128
                slot = (i % 2) * 2
                def f():
                    em.op("tensor", lambda e: e.matmul(
                        pA[:, qoff:512], B["KT"][0:64, kt, i * 128:(i + 1) * 128],
                        B["QTc"][0:64, c % 2, pr, qoff:512], start=True, stop=True,
                        skip_group_check=True),
                        waits=[em.m("pA_free"), em.m("pB_free"),
                               ("s", (phj or ph1)["s"]),
                               ("dqr", em.cnt["dqr"]), em.m("bias_v")])
                    em.op("tensor", lambda e: e.matmul(
                        pB[:, qoff:512], B["KT"][64:128, kt, i * 128:(i + 1) * 128],
                        B["QTc"][64:128, c % 2, pr, qoff:512], start=True, stop=True,
                        skip_group_check=True), inc=("t", 1))
                    em.setmark(f"st_t{i % 2}", "t")
                    em.op("scalar", lambda e: e.activation(
                        B["PT"][:, slot, qoff:512], pA[:, qoff:512], AF.Exp,
                        bias=B["bias_sb"][:, 0:1], scale=0.125),
                        waits=[em.m(f"st_t{i % 2}"), em.m(f"pt{slot}_free")],
                        inc=("s", 1))
                    em.setmark("pA_free", "s")
                    em.op("scalar", lambda e: e.activation(
                        B["PT"][:, slot + 1, qoff:512], pB[:, qoff:512], AF.Exp,
                        bias=B["bias_sb"][:, 0:1], scale=0.125),
                        waits=[em.m(f"pt{slot + 1}_free")], inc=("s", 1))
                    em.setmark("pB_free", "s")
                    em.setmark(f"exp_s{i % 2}", "s")
                    if i >= 4 * c:
                        em.op("vector", lambda e: e.tensor_tensor(
                            B["PT"][:, slot, qoff:qoff + 128],
                            B["PT"][:, slot, qoff:qoff + 128],
                            B["trim_s"][:], ALU.mult),
                            waits=[em.m(f"exp_s{i % 2}")], inc=("v", 1), lp=True)
                        em.op("vector", lambda e: e.tensor_tensor(
                            B["PT"][:, slot + 1, qoff:qoff + 128],
                            B["PT"][:, slot + 1, qoff:qoff + 128],
                            B["trim_s"][:], ALU.mult), inc=("v", 1), lp=True)
                        em.setmark(f"pm_v{i % 2}", "v")
                    else:
                        em.mark[f"pm_v{i % 2}"] = ("v", 0)
                return f

            def av(pr, i, kt):
                qoff = 0 if i < 4 * c else (i - 4 * c) * 128
                slot = (i % 2) * 2
                def f():
                    avw = [em.m(f"exp_s{i % 2}"), em.m(f"pm_v{i % 2}")]
                    if i == 0:
                        avw += [em.m("avt_v"), em.m("avt_s"),
                                ("s", (phj or ph1)["s"])]
                    em.op("tensor", lambda e: e.matmul(
                        pC[0:65, qoff:512], Vb[:, i, 2 * kt, 0:65],
                        B["PT"][:, slot, qoff:512],
                        start=(i == 0), stop=(i == nkv - 1), skip_group_check=True),
                        waits=avw)
                    em.op("tensor", lambda e: e.matmul(
                        pD[0:65, qoff:512], Vb[:, i, 2 * kt + 1, 0:65],
                        B["PT"][:, slot + 1, qoff:512],
                        start=(i == 0), stop=(i == nkv - 1), skip_group_check=True),
                        inc=("t", 1))
                    em.setmark(f"pt{slot}_free", "t")
                    em.setmark(f"pt{slot + 1}_free", "t")
                return f

            def tail_a(pr, last):
                def f():
                    em.setmark("av_t", "t")
                    em.op("scalar", lambda e: e.activation(
                        B["lnr"][0:1, 0:512], pC[64:65, 0:512], AF.Ln),
                        waits=[em.m("av_t"), em.m("rsb_free")], inc=("s", 1))
                    em.op("scalar", lambda e: e.activation(
                        B["rsb"][0:1, 0:512], B["lnr"][0:1, 0:512], AF.Exp, scale=-1.0),
                        waits=[("s", em.cnt["s"])], inc=("s", 1), lp=True)
                    em.op("scalar", lambda e: e.activation(
                        B["lnr"][0:1, 0:512], pD[64:65, 0:512], AF.Ln),
                        waits=[("s", em.cnt["s"])], inc=("s", 1))
                    em.op("scalar", lambda e: e.activation(
                        B["rsb"][0:1, 512:1024], B["lnr"][0:1, 0:512], AF.Exp,
                        scale=-1.0), waits=[("s", em.cnt["s"])], inc=("s", 1), lp=True)
                    em.setmark("rsb_s", "s")
                    em.op("vector", lambda e: e.tensor_copy(
                        B["attnraw"][0:64, :], pC[0:64, :]),
                        waits=[em.m("av_t"), em.m("araw_free")], inc=("v", 1))
                    em.op("vector", lambda e: e.tensor_copy(
                        B["attnraw"][64:128, :], pD[0:64, :]), inc=("v", 1))
                    em.setmark("araw_v", "v")
                    em.setmark("avt_v", "v")
                    em.setmark("avt_s", "s")
                    if last:
                        em.setmark(f"qtc_free{c % 2}", "t")
                        em.setmark("bias_used_s", "s")
                return f

            def tail_b(pr, last):
                def f():
                    em.op("tensor", lambda e: e.matmul(
                        pE[0:64, :], B["onesb"][0:1, 0:64], B["rsb"][0:1, 0:512],
                        start=True, stop=True, skip_group_check=True),
                        waits=[em.m("rsb_s"), em.m("pEfree")])
                    em.op("tensor", lambda e: e.matmul(
                        pE[64:128, :], B["onesb"][0:1, 0:64], B["rsb"][0:1, 512:1024],
                        start=True, stop=True, skip_group_check=True), inc=("t", 1))
                    em.setmark("pE_t", "t")
                    em.op("vector", lambda e: e.tensor_tensor(
                        B["attnT"][:, c % 2, pr, :], B["attnraw"][:], pE[:], ALU.mult),
                        waits=[em.m("araw_v"), em.m("pE_t"),
                               em.m(f"wo_done{c % 2}")], inc=("v", 1), lp=True)
                    em.setmark("pEfree", "v")
                    em.setmark("araw_free", "v")
                    em.setmark("rsb_free", "v")
                    if last:
                        em.setmark("attnT_v", "v")
                return f

            for pr in range(8):
                kt = 0 if pr < 4 else 1
                for i in range(nkv):
                    steps.append((st_exp(pr, i, kt), 1.0))
                    if i > 0:
                        steps.append((av(pr, i - 1, kt), 1.0))
                steps.append((av(pr, nkv - 1, kt), 1.0))
                steps.append((tail_a(pr, pr == 7), 2.0))
                steps.append((tail_b(pr, pr == 7), 1.5))
            return steps

        def wo_ln1_gate(c):
            steps = []
            def mk_wots(ts):
              def f():
                for dh in range(2):
                    pX = pF if dh == 0 else pG
                    fm = "pFfree" if dh == 0 else "pGfree"
                    tm = "pF_t" if dh == 0 else "pG_t"
                    def wogrp(e, ts=ts, dh=dh, pX=pX):
                        for j in range(8):
                            ins = e.matmul(pX[:],
                                           B["attnT"][:, c % 2, j, ts * 128:(ts + 1) * 128],
                                           Wos[:, j, dh * 512:(dh + 1) * 512],
                                           start=(j == 0), stop=(j == 7))
                        return ins
                    em.op("tensor", wogrp,
                          waits=[em.m("attnT_v"), em.m("w_attn"), em.m(fm)]
                          + ([("v", ph1["v"]), ("s", ph1["s"])] if c == 0 else []),
                          inc=("t", 1))
                    em.setmark(tm, "t")
                    if ts == 3 and dh == 1:
                        em.setmark(f"wo_done{c % 2}", "t")
                    em.op("vector", lambda e, ts=ts, dh=dh, pX=pX: e.tensor_tensor(
                        B["XM"][:, ts, dh * 512:(dh + 1) * 512], pX[:],
                        B["XM"][:, ts, dh * 512:(dh + 1) * 512], ALU.add),
                        waits=[em.m(tm), ("dx", em.cnt["dx"])], inc=("v", 1))
                    em.setmark(fm, "v")
              return f
            def mk_ln1(ts):
              def f():
                em.op("vector", lambda e, ts=ts: e.bn_stats(
                    B["stats"][:, 0:6], B["XM"][:, ts, 0:512]), inc=("v", 1))
                em.op("vector", lambda e, ts=ts: e.bn_stats(
                    B["stats"][:, 6:12], B["XM"][:, ts, 512:1024]), inc=("v", 1))
                em.op("vector", lambda e: e.bn_aggr(
                    B["stats"][:, 12:14], B["stats"][:, 0:12]),
                    waits=[("v", em.cnt["v"])], inc=("v", 1))
                em.op("vector", lambda e: e.tensor_scalar(
                    B["stats"][:, 14:15], B["stats"][:, 13:14], EPS, None, ALU.add),
                    waits=[("v", em.cnt["v"])], inc=("v", 1))
                em.setmark("lnv_v", "v")
                em.op("scalar", lambda e: e.activation(
                    B["stats"][:, 15:16], B["stats"][:, 14:15], AF.Ln),
                    waits=[em.m("lnv_v")], inc=("s", 1))
                em.op("scalar", lambda e: e.activation(
                    B["stats"][:, 14:15], B["stats"][:, 15:16], AF.Exp, scale=-0.5),
                    waits=[("s", em.cnt["s"])], inc=("s", 1))
                em.setmark("rstd_s", "s")
                em.op("vector", lambda e, ts=ts: e.tensor_scalar(
                    B["XM"][:, ts, :], B["XM"][:, ts, :], B["stats"][:, 12:13],
                    B["stats"][:, 14:15], ALU.subtract, ALU.mult),
                    waits=[em.m("rstd_s")], inc=("v", 1))
                em.setmark("x1ap_g", "v")
                em.op("scalar", lambda e, ts=ts: e.copy(
                    B["bst"][:], B["XM"][:, ts, :]),
                    waits=[em.m("bst_tfree"), em.m("x1ap_g")], inc=("s", 1))
                em.setmark("bst_v", "s")
              return f
            def mk_ln1tr(ts):
              def f():
                def x1tr(e):
                    for j in range(8):
                        ins = e.transpose(pH[:, j * 128:(j + 1) * 128],
                                          B["bst"][:, j * 128:(j + 1) * 128],
                                          B["idbf_s"][:])
                    return ins
                em.op("tensor", x1tr, waits=[em.m("bst_v"), em.m("pHfree")],
                      inc=("t", 1))
                em.setmark("pH_t", "t")
                em.setmark("bst_tfree", "t")
                em.op("vector", lambda e, ts=ts: e.tensor_copy(
                    B["X1T"][:, 0:8, ts * 128:(ts + 1) * 128],
                    pH[:].rearrange("p (j q) -> p j q", j=8)),
                    waits=[em.m("pH_t"), em.m("x1t_free")], inc=("v", 1))
                em.setmark("pHfree", "v")
                if ts == 3:
                    em.setmark("x1t_v", "v")
              return f
            def mk_gate(ts):
              def f():
                def ggrp(e, ts=ts):
                    for j in range(8):
                        e.matmul(pE[:, 0:E], B["X1T"][:, j, ts * 128:(ts + 1) * 128],
                                 Wgs[:, j, :], start=(j == 0), stop=False)
                    return e.matmul(pE[:, 0:E], B["onesb"][0:1, 0:128],
                                    B["bg_s"][0:1, :], start=False, stop=True)
                em.op("tensor", ggrp,
                      waits=[em.m("x1t_v"), em.m("pEfree"), em.m("w_attn")],
                      inc=("t", 1))
                em.setmark("pE_t", "t")
                g = B["gsc"]
                em.op("vector", lambda e: e.tensor_copy(g[:, 0:8], pE[:, 0:E]),
                      waits=[em.m("pE_t")], inc=("v", 1))
                em.setmark("pEfree", "v")
                em.op("vector", lambda e: e.reduce_max(g[:, 8:9], g[:, 0:8], axis=AX.X),
                      waits=[("v", em.cnt["v"])], inc=("v", 1))
                em.op("vector", lambda e: e.tensor_scalar(
                    g[:, 16:24], g[:, 0:8], g[:, 8:9], -1e30, ALU.is_ge, ALU.mult),
                    waits=[("v", em.cnt["v"])], inc=("v", 1))
                em.op("vector", lambda e: e.tensor_tensor(
                    g[:, 16:24], g[:, 16:24], g[:, 0:8], ALU.add),
                    waits=[("v", em.cnt["v"])], inc=("v", 1))
                em.op("vector", lambda e: e.reduce_max(g[:, 9:10], g[:, 16:24], axis=AX.X),
                      waits=[("v", em.cnt["v"])], inc=("v", 1))
                em.op("vector", lambda e: e.tensor_scalar(
                    g[:, 16:24], g[:, 0:8], g[:, 9:10], None, ALU.is_ge),
                    waits=[("v", em.cnt["v"])], inc=("v", 1))
                em.op("vector", lambda e, ts=ts: e.tensor_tensor(
                    B["wch"][:, ts, :], g[:, 16:24], g[:, 0:8], ALU.mult),
                    waits=[("v", em.cnt["v"])], inc=("v", 1))
                em.setmark("wch_v", "v")
              return f
            def mk_gatetr(ts):
              def f():
                em.op("tensor", lambda e, ts=ts: e.transpose(
                    pE[0:8, 0:128], B["wch"][:, ts, :], B["idf_s"][:]),
                    waits=[em.m("wch_v"), em.m("pEfree")], inc=("t", 1))
                em.setmark("pE_t", "t")
                em.op("vector", lambda e, ts=ts: e.tensor_copy(
                    B["wTb"][0:8, ts, :], pE[0:8, 0:128]),
                    waits=[em.m("pE_t"), ("dr", 16 * st["nrow"])], inc=("v", 1))
                em.setmark("pEfree", "v")
                if ts == 3:
                    em.setmark("wtb_v", "v")
              return f
            for ts in range(4):
                steps.append((mk_wots(ts), 4.0))
                steps.append((mk_ln1(ts), 4.0))
                steps.append((mk_ln1tr(ts), 2.0))
            for ts in range(4):
                steps.append((mk_gate(ts), 1.5))
                steps.append((mk_gatetr(ts), 1.0))
            return steps

        def ffn_steps(c):
            steps = []
            for dt in range(8):
                def b2init(dt=dt):
                    pX = pF if dt % 2 == 0 else pG
                    fm = "pFfree" if dt % 2 == 0 else "pGfree"
                    tm = "pF_t" if dt % 2 == 0 else "pG_t"
                    em.op("tensor", lambda e: e.matmul(
                        pX[:], B["b2_s"][0:8, dt * 128:(dt + 1) * 128],
                        B["wTb"][:].rearrange("p a b -> p (a b)"), start=True, stop=True),
                        waits=[em.m("wtb_v"), em.m(fm), em.m("w_attn")], inc=("t", 1))
                    em.setmark(tm, "t")
                    em.op("vector", lambda e: e.tensor_copy(
                        B["Yacc"][:, dt, :], pX[:]), waits=[em.m(tm)], inc=("v", 1))
                    em.setmark(fm, "v")
                steps.append((b2init, 1.0))
            for e_i in range(E):
                def loads_wb(e_i=e_i):
                    sl1 = st["wseq"] % 3
                    ph1g = ([("t", ph1["t"]), ("v", ph1["v"]), ("s", ph1["s"])]
                            if (c == 0 and e_i == 0) else [])
                    em.op("gpsimd", lambda e: e.dma_start(
                        out=B["Wsl"][:, sl1, :, :],
                        in_=dr["w1"][e_i].rearrange("(j p) h -> p j h", p=128)),
                        waits=[em.m(f"wsl{sl1}_free")] + ph1g, inc=("dw", 16))
                    em.mark[f"w1ld.{c}.{e_i}"] = ("dw", em.cnt["dw"])
                    em.mark[f"w1sl.{c}.{e_i}"] = sl1
                    st["wseq"] += 1
                    sl2 = st["wseq"] % 3
                    em.op("gpsimd", lambda e: e.dma_start(
                        out=B["Wsl"][:, sl2, :, :],
                        in_=dr["w2"][e_i].rearrange("(j p) h -> p j h", p=128)),
                        waits=[em.m(f"wsl{sl2}_free")], inc=("dw", 16))
                    em.mark[f"w2ld.{c}.{e_i}"] = ("dw", em.cnt["dw"])
                    em.mark[f"w2sl.{c}.{e_i}"] = sl2
                    st["wseq"] += 1
                    em.op("sync", lambda e: e.dma_start(
                        out=B["wrow"][0:1, e_i % 2, :],
                        in_=B["wTb"][e_i:e_i + 1, :, :].rearrange("p a b -> p (a b)")),
                        waits=[em.m("wtb_v"), em.m(f"wrow{e_i % 2}_free")],
                        inc=("dr", 16))
                    st["nrow"] += 1
                    em.op("tensor", lambda e: e.matmul(
                        pE[:], B["onesb"][0:1, 0:128], B["wrow"][0:1, e_i % 2, :],
                        start=True, stop=True, skip_group_check=True),
                        waits=[("dr", 16 * st["nrow"]), em.m("pEfree")], inc=("t", 1))
                    em.setmark("pE_t", "t")
                    em.setmark(f"wrow{e_i % 2}_free", "t")
                    em.op("vector", lambda e: e.tensor_copy(B["wb_sb"][:], pE[:]),
                          waits=[em.m("pE_t"), em.m("wb_free")], inc=("v", 1))
                    em.setmark("pEfree", "v")
                    em.setmark("wb_v", "v")
                steps.append((loads_wb, 1.0))
                for ht in range(8):
                    def hstep(e_i=e_i, ht=ht):
                        sl1 = em.mark[f"w1sl.{c}.{e_i}"]
                        pX = pF if ht % 2 == 0 else pG
                        fm = "pFfree" if ht % 2 == 0 else "pGfree"
                        tm = "pF_t" if ht % 2 == 0 else "pG_t"
                        def hgrp(e):
                            for j in range(8):
                                ins = e.matmul(pX[:],
                                               B["Wsl"][:, sl1, j, ht * 128:(ht + 1) * 128],
                                               B["X1T"][:, j, :], start=(j == 0), stop=(j == 7))
                            return ins
                        em.op("tensor", hgrp,
                              waits=[em.m(f"w1ld.{c}.{e_i}"), em.m("x1t_v"), em.m(fm)],
                              inc=("t", 1))
                        em.setmark(tm, "t")
                        if ht == 7:
                            em.setmark(f"wsl{sl1}_free", "t")
                        em.op("scalar", lambda e: e.activation(
                            B["Hb"][:, ht % 2, :], pX[:], AF.Relu,
                            bias=B["b1r_s"][:, e_i * 8 + ht:e_i * 8 + ht + 1]),
                            waits=[em.m(tm), ("dc", 16 * 5), em.m(f"hb{ht % 2}_free")],
                            inc=("s", 1))
                        em.setmark(fm, "s")
                        em.setmark("hb_s", "s")
                        em.op("vector", lambda e: e.tensor_tensor(
                            B["Hw"][:, ht, :], B["Hb"][:, ht % 2, :], B["wb_sb"][:],
                            ALU.mult),
                            waits=[em.m("hb_s"), em.m("wb_v"), em.m("hw_free")],
                            inc=("v", 1), lp=True)
                        em.setmark(f"hb{ht % 2}_free", "v")
                        if ht == 7:
                            em.setmark("hw_v", "v")
                            em.setmark("wb_free", "v")
                    steps.append((hstep, 2.0))
                for dt in range(8):
                    def ystep(e_i=e_i, dt=dt):
                        sl2 = em.mark[f"w2sl.{c}.{e_i}"]
                        pX = pF if dt % 2 == 0 else pG
                        fm = "pFfree" if dt % 2 == 0 else "pGfree"
                        tm = "pF_t" if dt % 2 == 0 else "pG_t"
                        def ygrp(e):
                            for ht in range(8):
                                ins = e.matmul(pX[:],
                                               B["Wsl"][:, sl2, ht, dt * 128:(dt + 1) * 128],
                                               B["Hw"][:, ht, :], start=(ht == 0), stop=(ht == 7))
                            return ins
                        em.op("tensor", ygrp,
                              waits=[em.m(f"w2ld.{c}.{e_i}"), em.m("hw_v"), em.m(fm)],
                              inc=("t", 1))
                        em.setmark(tm, "t")
                        if dt == 7:
                            em.setmark(f"wsl{sl2}_free", "t")
                            em.setmark("hw_free", "t")
                        em.op("vector", lambda e: e.tensor_tensor(
                            B["Yacc"][:, dt, :], pX[:], B["Yacc"][:, dt, :], ALU.add),
                            waits=[em.m(tm)], inc=("v", 1))
                        em.setmark(fm, "v")
                    steps.append((ystep, 2.0))
            return steps

        def ln2_out(c):
            steps = []
            def mk_ln2a(ts):
              def f():
                em.op("scalar", lambda e, ts=ts: e.copy(
                    B["bst"][:].rearrange("p (a b) -> p a b", a=8),
                    B["Yacc"][:, :, ts * 128:(ts + 1) * 128]),
                    waits=[em.m("bst_tfree"), ("v", em.cnt["v"])], inc=("s", 1))
                em.setmark("ybf_v", "s")
              return f
            def mk_ln2(ts):
              def f():
                zs = ts % 2
                def ytr(e):
                    for dt in range(8):
                        ins = e.transpose(pH[:, dt * 128:(dt + 1) * 128],
                                          B["bst"][:, dt * 128:(dt + 1) * 128],
                                          B["idbf_s"][:])
                    return ins
                em.op("tensor", ytr, waits=[em.m("ybf_v"), em.m("pHfree")],
                      inc=("t", 1))
                em.setmark("pH_t", "t")
                em.setmark("bst_tfree", "t")
                em.op("vector", lambda e, ts=ts, zs=zs: e.tensor_tensor(
                    B["z2"][:, zs, :], pH[:], B["XM"][:, ts, :], ALU.add),
                    waits=[em.m("pH_t"), ("do", 16 * st["z2war"][zs])], inc=("v", 1))
                em.setmark("pHfree", "v")
                em.op("vector", lambda e, zs=zs: e.bn_stats(
                    B["stats"][:, 0:6], B["z2"][:, zs, 0:512]), inc=("v", 1))
                em.op("vector", lambda e, zs=zs: e.bn_stats(
                    B["stats"][:, 6:12], B["z2"][:, zs, 512:1024]), inc=("v", 1))
                em.op("vector", lambda e: e.bn_aggr(
                    B["stats"][:, 12:14], B["stats"][:, 0:12]),
                    waits=[("v", em.cnt["v"])], inc=("v", 1))
                em.op("vector", lambda e: e.tensor_scalar(
                    B["stats"][:, 14:15], B["stats"][:, 13:14], EPS, None, ALU.add),
                    waits=[("v", em.cnt["v"])], inc=("v", 1))
                em.setmark("lnv_v", "v")
                em.op("scalar", lambda e: e.activation(
                    B["stats"][:, 15:16], B["stats"][:, 14:15], AF.Ln),
                    waits=[em.m("lnv_v")], inc=("s", 1))
                em.op("scalar", lambda e: e.activation(
                    B["stats"][:, 14:15], B["stats"][:, 15:16], AF.Exp, scale=-0.5),
                    waits=[("s", em.cnt["s"])], inc=("s", 1))
                em.setmark("rstd_s", "s")
                em.op("vector", lambda e, zs=zs: e.tensor_scalar(
                    B["z2"][:, zs, :], B["z2"][:, zs, :], B["stats"][:, 12:13],
                    B["stats"][:, 14:15], ALU.subtract, ALU.mult),
                    waits=[em.m("rstd_s")], inc=("v", 1))
                em.setmark("x2_v", "v")
                em.op("sync", lambda e, c=c, ts=ts, zs=zs: e.dma_start(
                    out=out[c * CH + ts * 128: c * CH + (ts + 1) * 128, :],
                    in_=B["z2"][:, zs, :]),
                    waits=[em.m("x2_v")], inc=("do", 16))
                st["nout"] += 1
                st["z2war"][zs] = st["nout"]
                if ts == 3:
                    em.setmark("xm_free", "v")
                    em.setmark("x1t_free", "t")
              return f
            for ts in range(4):
                steps.append((mk_ln2a(ts), 1.5))
                steps.append((mk_ln2(ts), 5.0))
            return steps

        def weave(A, F):
            wa = sum(w for _, w in A) or 1e-9
            wf = sum(w for _, w in F) or 1e-9
            ia = ifl = 0
            ca = cf = 0.0
            while ia < len(A) or ifl < len(F):
                if ifl < len(F) and (ia >= len(A) or cf / wf <= ca / wa):
                    F[ifl][0]()
                    cf += F[ifl][1]
                    ifl += 1
                else:
                    A[ia][0]()
                    ca += A[ia][1]
                    ia += 1

        weave([], ph1_steps(0))
        ph1c0.update(em.cnt)
        attn_prologue(0, ph1c0)
        rest = []
        for c_ in range(1, NCH_):
            rest += ph1_steps(c_)
        weave(attn_steps(0, ph1c0), rest)
        ph1.update(em.cnt)
        dump("d_xt", B["XTc"][:, (S // CH - 1) % 2, :, :])
        dump("d_qst", B["QTstage"][:])
        dump("d_kt", KT[:])
        dump("d_vb", Vb[:])
        dump("d_km", B["kmax2"][:])
        dump("d_qm", B["qmax2"][:])
        em.op("sync", lambda e: e.dma_start(
            out=B["XM"][:],
            in_=dr["x"][0:CH, :].rearrange("(ts p) d -> p ts d", p=128)),
            waits=[("t", ph1["t"]), ("v", ph1["v"]), ("s", ph1["s"]),
                   ("dx", em.cnt["dx"])], inc=("dx", 16))
        for c in range(NCH_):
            def xm_load(c=c):
                em.op("sync", lambda e: e.dma_start(
                    out=B["XM"][:],
                    in_=dr["x"][(c + 1) * CH:(c + 2) * CH, :].rearrange(
                        "(ts p) d -> p ts d", p=128)),
                    waits=[em.m("xm_free")], inc=("dx", 16))
            F = wo_ln1_gate(c)
            if c + 1 < NCH_:
                attn_prologue(c + 1)
                F = F + ffn_steps(c) + ln2_out(c) + [(xm_load, 0.5)]
                weave(attn_steps(c + 1), F)
            else:
                F = F + ffn_steps(c) + ln2_out(c)
                weave([], F)

        dump("d_qtc", B["QTc"][:, (NCH_ - 1) % 2, :, :])
        dump("d_attnt", B["attnT"][:, (NCH_ - 1) % 2, :, :])
        dump("d_x1", B["XM"][:])
        dump("d_x1t", B["X1T"][:])
        dump("d_w", B["wch"][:])
        dump("d_yacc", B["Yacc"][:])
        dump("d_bias", B["bias_sb"][:])
        dump("d_pt", B["PT"][:])

        em.op("sync", lambda e: e.wait_ge(sems["do"], 16 * st["nout"]))

    with ctx:
        with nc.Block() as block:
            for name, attr in (("sync", block.sync), ("gpsimd", block.gpsimd),
                               ("tensor", block.tensor), ("vector", block.vector),
                               ("scalar", block.scalar)):
                def body(eng, name=name):
                    sched(Em(nc, sems, name, eng, serialize=serialize))
                attr(body)
    return nc


def host_inputs(x_core, Wq, Wk, Wv, Wo, Wg, bg, W1, b1, W2, b2, S):
    wq_p = Wq.reshape(DM, NH, DH)[:, PERM, :].reshape(DM, DM)
    wo_p = Wo.reshape(NH, DH, DM)[PERM].reshape(DM, DM)
    b1r = b1.reshape(E, 8, 128).transpose(2, 0, 1).reshape(128, 64)
    C, Sg = rope_tables(S)
    idn = np.eye(128, dtype=np.float32)
    tri = (np.arange(128)[:, None] <= np.arange(128)[None, :]).astype(np.float32)
    psw = np.zeros((128, 128), np.float32)
    for k in range(128):
        psw[k, k ^ 1] = 1.0
    return {
        "x": np.ascontiguousarray(x_core, dtype=np.float32),
        "wq": np.ascontiguousarray(wq_p, dtype=np.float32),
        "wk": np.ascontiguousarray(Wk, dtype=np.float32),
        "wv": np.ascontiguousarray(Wv, dtype=np.float32),
        "wo": np.ascontiguousarray(wo_p, dtype=np.float32),
        "wg": np.ascontiguousarray(Wg, dtype=np.float32),
        "bg": np.ascontiguousarray(bg.reshape(1, E), dtype=np.float32),
        "w1": np.ascontiguousarray(W1, dtype=np.float32),
        "b1r": np.ascontiguousarray(b1r, dtype=np.float32),
        "w2": np.ascontiguousarray(W2, dtype=np.float32),
        "b2": np.ascontiguousarray(b2, dtype=np.float32),
        "ropec": C,
        "ropes": Sg,
        "idbf": idn.astype(BF),
        "idf": idn,
        "trim": tri.astype(BF),
        "pswap": psw.astype(BF),
    }


def kernel(L0_attn_hidden_states, L0_attn_Wq, L0_attn_Wk, L0_attn_Wv, L0_attn_Wo,
           L0_ffn_W_gate, L0_ffn_b_gate, L0_ffn_W1, L0_ffn_b1, L0_ffn_W2, L0_ffn_b2):
    x = np.asarray(L0_attn_hidden_states, dtype=np.float32)
    Bn, S, _ = x.shape
    args = [np.asarray(a, dtype=np.float32) for a in
            (L0_attn_Wq, L0_attn_Wk, L0_attn_Wv, L0_attn_Wo, L0_ffn_W_gate,
             L0_ffn_b_gate, L0_ffn_W1, L0_ffn_b1, L0_ffn_W2, L0_ffn_b2)]
    nc = build_nc(S)
    in_maps = [host_inputs(x[b], *args, S) for b in range(Bn)]
    res = run_bass_kernel_spmd(nc, in_maps, core_ids=list(range(Bn)))
    return np.stack([np.asarray(r["out"]) for r in res.results], axis=0)


# version 14
# speedup vs baseline: 1.0047x; 1.0047x over previous
import numpy as np
import ml_dtypes
import concourse.bass as bass
import concourse.mybir as mybir
from concourse.bass_utils import run_bass_kernel_spmd
from contextlib import ExitStack

F32 = mybir.dt.float32
BF16 = mybir.dt.bfloat16
AF = mybir.ActivationFunctionType
ALU = mybir.AluOpType
AX = mybir.AxisListType
BF = ml_dtypes.bfloat16

DM = 1024
NH = 16
HKV = 4
DH = 64
E = 8
CH = 512
EPS = 1e-5
PERM = [0, 4, 1, 5, 2, 6, 3, 7, 8, 12, 9, 13, 10, 14, 11, 15]

SEMS = ["dc", "dw", "dx", "dq", "dqr", "dr", "do", "t", "v", "s", "g"]


def rope_tables(S):
    inv = 1.0 / (10000.0 ** (np.arange(0, 64, 2, dtype=np.float64) / 64.0))
    fr = np.arange(S, dtype=np.float64)[:, None] * inv[None, :]
    emb = np.concatenate([fr, fr], axis=1)
    cos = np.cos(emb)[:, 0::2]
    sin = np.sin(emb)[:, 0::2]
    C = np.empty((128, S), np.float64)
    Sg = np.empty((128, S), np.float64)
    for r in range(128):
        dl = r % 64
        pair = dl // 2
        C[r] = cos[:, pair]
        Sg[r] = sin[:, pair] * (-1.0 if dl % 2 == 0 else 1.0)
    return C.astype(np.float32), Sg.astype(np.float32)


class Em:

    def __init__(self, nc, sems, name=None, eng=None, serialize=False):
        self.nc = nc
        self.sems = sems
        self.cnt = {k: 0 for k in sems}
        self.name = name
        self.eng = eng
        self.lastw = {}
        self.mark = {}
        self.serialize = serialize

    def setmark(self, key, sem):
        self.mark[key] = (sem, self.cnt[sem])

    def m(self, key):
        return self.mark.get(key, ("v", 0))

    def op(self, engine, build, inc=None, waits=(), lp=False):
        if self.name == engine:
            if self.serialize:
                waits = list(waits) + [(s, self.cnt[s]) for s in self.sems]
            for (s, v) in waits:
                if v <= 0:
                    continue
                if v > self.lastw.get(s, 0):
                    self.eng.wait_ge(self.sems[s], v)
                    self.lastw[s] = v
            if lp:
                with self.nc.allow_low_precision("bf16 matmul operand prep"):
                    ins = build(self.eng)
            else:
                ins = build(self.eng)
            if inc is not None and ins is not None:
                ins.then_inc(self.sems[inc[0]], inc[1])
        if inc is not None:
            self.cnt[inc[0]] += inc[1]


def build_nc(S, dbg=False, serialize=False):
    NCH = S // CH
    nc = bass.Bass("TRN2", target_bir_lowering=False)

    dr = {}
    def di(name, shape, dt):
        dr[name] = nc.dram_tensor(name, list(shape), dt, kind="ExternalInput").ap()
    di("x", (S, DM), F32)
    di("wq", (DM, DM), F32)
    di("wk", (DM, 256), F32)
    di("wv", (DM, 256), F32)
    di("wo", (DM, DM), F32)
    di("wg", (DM, E), F32)
    di("bg", (1, E), F32)
    di("w1", (E, DM, DM), F32)
    di("b1r", (128, 64), F32)
    di("w2", (E, DM, DM), F32)
    di("b2", (E, DM), F32)
    di("ropec", (128, S), F32)
    di("ropes", (128, S), F32)
    di("idbf", (128, 128), BF16)
    di("idf", (128, 128), F32)
    di("trim", (128, 128), BF16)
    di("pswap", (128, 128), BF16)
    out = nc.dram_tensor("out", [S, DM], F32, kind="ExternalOutput").ap()
    qspill = nc.dram_tensor("qspill", [NCH, 128, 8 * CH], BF16, kind="Internal").ap()
    dbgt = {}
    if dbg:
        for nm, shape, dt in [
            ("d_xt", (128, 8, CH), BF16), ("d_qst", (128, 8, CH), BF16),
            ("d_kt", (128, 2, S), BF16), ("d_vb", (128, S // 128, 4, 66), BF16),
            ("d_km", (1, 4), F32), ("d_qm", (1, NCH), F32),
            ("d_qtc", (128, 8, CH), BF16), ("d_attnt", (128, 8, CH), BF16),
            ("d_x1", (128, 4, DM), F32), ("d_x1t", (128, 8, CH), BF16),
            ("d_w", (128, 4, E), F32), ("d_yacc", (128, 8, CH), F32),
            ("d_bias", (128, 1), F32), ("d_pt", (128, 4, CH), BF16),
        ]:
            dbgt[nm] = nc.dram_tensor(nm, list(shape), dt, kind="ExternalOutput").ap()

    B = {}
    def sba(name, shape, dt):
        B[name] = nc.alloc_sbuf_tensor(name, list(shape), dt)
        return B[name]

    NT = S // 128
    sba("KT", (128, 2, S), BF16)
    sba("Vb", (128, NT, 4, 66), BF16)
    sba("Wos", (128, 8, DM), BF16)
    sba("Wgs", (128, 8, E), BF16)
    sba("b1r_s", (128, 64), F32)
    sba("b2_s", (E, DM), BF16)
    sba("bg_s", (1, E), BF16)
    sba("idbf_s", (128, 128), BF16)
    sba("idf_s", (128, 128), F32)
    sba("trim_s", (128, 128), BF16)
    sba("pswap_s", (128, 128), BF16)
    sba("onesb", (1, 128), BF16)
    sba("onescol", (128, 1), BF16)
    sba("kmax2", (1, 4), F32)
    sba("qmax2", (1, NCH), F32)
    sba("bias_sb", (128, 1), F32)
    sba("cscr", (1, 4), F32)
    sba("cscrb", (1, 1), BF16)
    base0 = nc.sbuf_base
    sba("Wqs", (128, 8, DM), BF16)
    sba("Wks", (128, 8, 256), BF16)
    sba("Wvs", (128, 8, 256), BF16)
    sba("ropecs", (128, 2, CH), F32)
    sba("Xb", (128, 2, 4, DM), BF16)
    sba("XTc", (128, 2, 8, CH), BF16)
    sba("QTstage", (128, 8, CH), BF16)
    sba("Qev", (128, 2, CH), BF16)
    sba("Qsh", (128, 2, CH), BF16)
    sba("scr1", (128, 2, CH), F32)
    sba("scr2", (128, 2, CH), F32)
    sba("sqscr", (128, 2, CH), BF16)
    sba("qns", (1, CH), F32)
    ph1_top = nc.sbuf_base
    nc.sbuf_base = base0
    sba("XM", (128, 4, DM), F32)
    sba("z2", (128, 2, DM), F32)
    sba("bst", (128, DM), BF16)
    sba("X1T", (128, 8, CH), BF16)
    sba("stats", (128, 16), F32)
    sba("gsc", (128, 32), F32)
    sba("wch", (128, 4, E), F32)
    sba("wTb", (E, 4, 128), BF16)
    sba("wrow", (1, 2, CH), BF16)
    sba("wb_sb", (128, CH), BF16)
    sba("Hb", (128, 2, CH), BF16)
    sba("Hw", (128, 8, CH), BF16)
    sba("Yacc", (128, 8, CH), F32)
    sba("Wsl", (128, 3, 8, DM), BF16)
    nc.sbuf_base = max(nc.sbuf_base, ph1_top)
    sba("QTc", (128, 2, 8, CH), BF16)
    sba("attnT", (128, 2, 8, CH), BF16)
    sba("attnraw", (128, CH), BF16)
    sba("rsb", (1, 2 * CH), BF16)
    sba("lnr", (1, CH), F32)
    sba("PT", (128, 4, CH), BF16)

    ctx = ExitStack()
    e_ = ctx.enter_context
    def ps(name, shape, dt):
        B[name] = e_(nc.psum_tensor(name, list(shape), dt))
        return B[name]
    ps("pA", (128, 512), F32)
    ps("pB", (128, 512), F32)
    ps("pC", (128, 512), F32)
    ps("pD", (128, 512), F32)
    ps("pE", (128, 512), F32)
    ps("pF", (128, 512), F32)
    ps("pG", (128, 512), F32)
    ps("pH", (128, 1024), BF16)

    sems = {k: e_(nc.semaphore(k)) for k in SEMS}

    def sched(em):
        KT, Vb, Wos, Wgs = (B[k] for k in ("KT", "Vb", "Wos", "Wgs"))
        pA, pB, pC, pD, pE, pF, pG, pH = (B[k] for k in
            ("pA", "pB", "pC", "pD", "pE", "pF", "pG", "pH"))

        for nm in ("idbf", "idf", "trim", "pswap", "b1r"):
            dst = B[nm + "_s"]
            src = dr[nm]
            em.op("sync", lambda e, d=dst, s=src: e.dma_start(out=d[:], in_=s),
                  inc=("dc", 16))
        em.op("gpsimd", lambda e: e.dma_start(
            out=B["Wqs"][:], in_=dr["wq"].rearrange("(j p) d -> p j d", p=128)), inc=("dw", 16))
        em.op("gpsimd", lambda e: e.dma_start(
            out=B["Wks"][:], in_=dr["wk"].rearrange("(j p) d -> p j d", p=128)), inc=("dw", 16))
        em.op("gpsimd", lambda e: e.dma_start(
            out=B["Wvs"][:], in_=dr["wv"].rearrange("(j p) d -> p j d", p=128)), inc=("dw", 16))
        em.op("gpsimd", lambda e: e.dma_start(
            out=Wos[:], in_=dr["wo"].rearrange("(j p) d -> p j d", p=128)), inc=("dw", 16))
        em.op("gpsimd", lambda e: e.dma_start(
            out=Wgs[:], in_=dr["wg"].rearrange("(j p) d -> p j d", p=128)), inc=("dw", 16))
        em.op("gpsimd", lambda e: e.dma_start(out=B["b2_s"][:], in_=dr["b2"]), inc=("dw", 16))
        em.op("gpsimd", lambda e: e.dma_start(out=B["bg_s"][:], in_=dr["bg"]), inc=("dw", 16))
        em.setmark("w_attn", "dw")
        em.op("vector", lambda e: e.memset(B["onesb"][:], 1.0), inc=("v", 1))
        em.op("vector", lambda e: e.memset(B["onescol"][:], 1.0), inc=("v", 1))
        em.op("vector", lambda e: e.memset(
            Vb[:].rearrange("p a b c -> p (a b c)"), 1.0), inc=("v", 1))
        em.op("vector", lambda e: e.memset(B["kmax2"][:], 0.0), inc=("v", 1))

        def ph1_steps(c):
            steps = []
            def step(f, w=1.0):
                steps.append((f, w))
            cc = slice(c * CH, (c + 1) * CH)
            par = c % 2
            step(lambda: em.op("gpsimd", lambda e, c=c, par=par: e.dma_start(
                out=B["Xb"][:, par, :, :],
                in_=dr["x"][c * CH:(c + 1) * CH, :].rearrange("(ts p) d -> p ts d", p=128)),
                waits=[em.m(f"xbfree{par}")], inc=("dx", 16)), 0.3)
            def mk_tr(ts, nld):
              def f():
                def tr(e, par=par, ts=ts):
                    for j in range(8):
                        ins = e.transpose(pH[:, j * 128:(j + 1) * 128],
                                          B["Xb"][:, par, ts, j * 128:(j + 1) * 128],
                                          B["idbf_s"][:])
                    return ins
                em.op("tensor", tr,
                      waits=[("dx", 16 * nld), ("dc", 16), em.m("pHfree"),
                             em.m(f"xt_free{par}")],
                      inc=("t", 1))
                em.setmark("pH_t", "t")
                if ts == 3:
                    em.setmark(f"xbfree{par}", "t")
                em.op("scalar", lambda e, par=par, ts=ts: e.copy(
                    B["XTc"][:, par, 0:8, ts * 128:(ts + 1) * 128],
                    pH[:].rearrange("p (j q) -> p j q", j=8)),
                    waits=[em.m("pH_t")], inc=("s", 1))
                em.setmark("pHfree", "s")
                if ts == 3:
                    em.setmark("xtc_ready", "s")
              return f
            nld = em.ph1_nld = getattr(em, "ph1_nld", 0) + 1
            for ts in range(4):
                step(mk_tr(ts, nld), 1.2)
            def ropeld():
                em.op("sync", lambda e, c=c: e.dma_start(
                    out=B["ropecs"][:, 0, :], in_=dr["ropec"][:, c * CH:(c + 1) * CH]),
                    waits=[em.m("ropefree")], inc=("dc", 16))
                em.op("sync", lambda e, c=c: e.dma_start(
                    out=B["ropecs"][:, 1, :], in_=dr["ropes"][:, c * CH:(c + 1) * CH]),
                    inc=("dc", 16))
                em.setmark("rope_ld", "dc")
            step(ropeld, 0.2)

            projs = ([("v", ts) for ts in range(4)] + [("k", 0), ("k", 1)]
                     + [("q", jq) for jq in range(8)])
            def dst_ap(kind, idx, c=c):
                if kind == "k":
                    return B["KT"][:, idx, c * CH:(c + 1) * CH]
                return B["QTstage"][:, idx, :]

            def stage0(j):
                kind, idx = projs[j]
                pp = j % 2
                pX = pF if pp == 0 else pG
                fm = "pFfree" if pp == 0 else "pGfree"
                tm = "pF_t" if pp == 0 else "pG_t"
                if kind == "v":
                    def vgrp(e, par=par, idx=idx, pX=pX):
                        for jj in range(8):
                            ins = e.matmul(pX[:, 0:256],
                                           B["XTc"][:, par, jj, idx * 128:(idx + 1) * 128],
                                           B["Wvs"][:, jj, :], start=(jj == 0), stop=(jj == 7))
                        return ins
                    em.op("tensor", vgrp,
                          waits=[em.m("xtc_ready"), em.m("w_attn"), em.m(fm)],
                          inc=("t", 1))
                    em.setmark(tm, "t")
                    em.op("scalar", lambda e, c=c, idx=idx, pX=pX: e.copy(
                        Vb[:, c * 4 + idx, :, 0:64],
                        pX[:, 0:256].rearrange("p (h d) -> p h d", h=4)),
                        waits=[em.m(tm)], inc=("s", 1))
                    em.setmark(fm, "s")
                    return
                lhsrc = B["Wks"] if kind == "k" else B["Wqs"]
                def grp(e, par=par, kind=kind, idx=idx, pX=pX):
                    for jj in range(8):
                        ins = e.matmul(pX[:, :], lhsrc[:, jj, idx * 128:(idx + 1) * 128],
                                       B["XTc"][:, par, jj, :],
                                       start=(jj == 0), stop=(jj == 7))
                    return ins
                em.op("tensor", grp,
                      waits=[em.m("xtc_ready"), em.m("w_attn"), em.m(fm)],
                      inc=("t", 1))
                em.setmark(tm, "t")
                em.op("scalar", lambda e, pp=pp, pX=pX: e.copy(B["Qev"][:, pp, :], pX[:]),
                      waits=[em.m(tm), ("dq", 16 * c), em.m(f"qevrd{pp}"),
                             em.m(f"qev_t{pp}")],
                      inc=("s", 1))
                em.setmark(fm, "s")
                em.setmark(f"qev_s{pp}", "s")
                em.op("tensor", lambda e, pp=pp: e.matmul(
                    pC[:], B["pswap_s"][:], B["Qev"][:, pp, :], start=True, stop=True),
                    waits=[em.m(f"qev_s{pp}"), em.m("pCfree"), ("dc", 16 * 4)],
                    inc=("t", 1))
                em.setmark("pC_t", "t")
                em.setmark(f"qev_t{pp}", "t")
                em.op("scalar", lambda e, pp=pp: e.copy(B["Qsh"][:, pp, :], pC[:]),
                      waits=[em.m("pC_t"), em.m(f"qshrd{pp}")], inc=("s", 1))
                em.setmark("pCfree", "s")
                em.setmark(f"qsh_s{pp}", "s")

            def stage1(j):
                kind, idx = projs[j]
                if kind == "v":
                    return
                pp = j % 2
                em.op("vector", lambda e, pp=pp: e.tensor_tensor(
                    B["scr1"][:, pp, :], B["Qev"][:, pp, :], B["ropecs"][:, 0, :],
                    ALU.mult),
                    waits=[em.m("rope_ld"), em.m(f"qev_s{pp}")], inc=("v", 1))
                em.setmark(f"qevrd{pp}", "v")
                em.op("vector", lambda e, pp=pp: e.tensor_tensor(
                    B["scr2"][:, pp, :], B["Qsh"][:, pp, :], B["ropecs"][:, 1, :],
                    ALU.mult),
                    waits=[em.m(f"qsh_s{pp}")], inc=("v", 1))
                em.setmark(f"qshrd{pp}", "v")
                em.op("vector", lambda e, kind=kind, idx=idx, pp=pp: e.tensor_tensor(
                    dst_ap(kind, idx), B["scr1"][:, pp, :], B["scr2"][:, pp, :],
                    ALU.add), inc=("v", 1), lp=True)
                em.op("vector", lambda e, kind=kind, idx=idx, pp=pp: e.tensor_tensor(
                    B["sqscr"][:, pp, :], dst_ap(kind, idx), dst_ap(kind, idx),
                    ALU.mult),
                    waits=[em.m(f"sq_t{pp}")], inc=("v", 1), lp=True)
                em.setmark(f"rot_v{pp}", "v")

            def stage2(j):
                kind, idx = projs[j]
                if kind == "v":
                    return
                pp = j % 2
                em.op("tensor", lambda e, pp=pp: e.matmul(
                    pE[0:1, :], B["onescol"][:], B["sqscr"][:, pp, :],
                    start=True, stop=True, skip_group_check=True),
                    waits=[em.m(f"rot_v{pp}"), em.m("pEfree")], inc=("t", 1))
                em.setmark("pE_t", "t")
                em.setmark(f"sq_t{pp}", "t")
                if idx == 0:
                    em.op("vector", lambda e: e.tensor_copy(
                        B["qns"][0:1, :], pE[0:1, :]),
                        waits=[em.m("pE_t")], inc=("v", 1))
                else:
                    em.op("vector", lambda e: e.tensor_tensor(
                        B["qns"][0:1, :], B["qns"][0:1, :], pE[0:1, :], ALU.add),
                        waits=[em.m("pE_t")], inc=("v", 1))
                em.setmark("pEfree", "v")
                if kind == "k" and idx == 1:
                    em.op("vector", lambda e: e.reduce_max(
                        B["kmax2"][0:1, 1:2], B["qns"][0:1, :], axis=AX.X),
                        waits=[("v", em.cnt["v"])], inc=("v", 1))
                    em.op("vector", lambda e: e.tensor_tensor(
                        B["kmax2"][0:1, 0:1], B["kmax2"][0:1, 0:1],
                        B["kmax2"][0:1, 1:2], ALU.max),
                        waits=[("v", em.cnt["v"])], inc=("v", 1))
                if kind == "q" and idx == 7:
                    em.op("vector", lambda e, c=c: e.reduce_max(
                        B["qmax2"][0:1, c:c + 1], B["qns"][0:1, :], axis=AX.X),
                        waits=[("v", em.cnt["v"])], inc=("v", 1))

            def mk_pipe(j):
                def f():
                    if j < len(projs):
                        stage0(j)
                    if 1 <= j <= len(projs):
                        stage1(j - 1)
                    if j >= 2:
                        stage2(j - 2)
                return f
            for j in range(len(projs) + 2):
                step(mk_pipe(j), 1.6)
            def fin():
                em.setmark("ropefree", "v")
                em.setmark("qt_v", "v")
                em.op("sync", lambda e, c=c: e.dma_start(
                    out=qspill[c], in_=B["QTstage"][:].rearrange("p a b -> p (a b)")),
                    waits=[em.m("qt_v")], inc=("dq", 16))
                em.setmark(f"xt_free{par}", "t")
            step(fin, 0.2)
            return steps

        NCH_ = S // CH

        def dump(nm, buf):
            if not dbg:
                return
            em.op("sync", lambda e, nm=nm, buf=buf: e.dma_start(
                out=dbgt[nm], in_=buf),
                waits=[("t", em.cnt["t"]), ("v", em.cnt["v"]), ("s", em.cnt["s"]),
                       ("dw", em.cnt["dw"]), ("dx", em.cnt["dx"]),
                       ("dqr", em.cnt["dqr"]), ("dr", em.cnt["dr"])],
                inc=("dc", 16))

        ph1 = {}
        ph1c0 = {}

        st = {"nout": 0, "wseq": 0, "nrow": 0, "z2war": [0, 0]}

        def attn_prologue(c, phj=None):
            joinw = ([("t", phj["t"]), ("v", phj["v"]), ("s", phj["s"])]
                     if phj is not None else [])
            em.op("sync", lambda e, c=c: e.dma_start(
                out=B["QTc"][:, c % 2, :, :].rearrange("p a b -> p (a b)"),
                in_=qspill[c]),
                waits=[("dq", 16 * (c + 1)), em.m(f"qtc_free{c % 2}")] + joinw,
                inc=("dqr", 16))
            em.op("vector", lambda e, c=c: e.tensor_tensor(
                B["cscr"][0:1, 0:1], B["qmax2"][0:1, c:c + 1], B["kmax2"][0:1, 0:1],
                ALU.mult), waits=[em.m("bias_used_s"), ("v", em.cnt["v"])],
                inc=("v", 1))
            em.setmark("cb_v", "v")
            em.op("scalar", lambda e: e.activation(
                B["cscr"][0:1, 1:2], B["cscr"][0:1, 0:1], AF.Ln),
                waits=[em.m("cb_v")], inc=("s", 1))
            em.op("scalar", lambda e: e.activation(
                B["cscr"][0:1, 2:3], B["cscr"][0:1, 1:2], AF.Exp, scale=0.5),
                waits=[("s", em.cnt["s"])], inc=("s", 1))
            em.setmark("cb_s", "s")
            em.op("vector", lambda e: e.tensor_copy(B["cscrb"][0:1, 0:1],
                                                    B["cscr"][0:1, 2:3]),
                  waits=[em.m("cb_s")], inc=("v", 1))
            em.setmark("cbb_v", "v")
            em.op("tensor", lambda e: e.matmul(
                pE[:, 0:1], B["onesb"][:], B["cscrb"][0:1, 0:1], start=True, stop=True,
                skip_group_check=True),
                waits=[em.m("cbb_v"), em.m("pEfree")] + joinw, inc=("t", 1))
            em.setmark("pE_t", "t")
            em.op("vector", lambda e: e.tensor_scalar(
                B["bias_sb"][:], pE[:, 0:1], -0.125, None, ALU.mult),
                waits=[em.m("pE_t")], inc=("v", 1))
            em.setmark("pEfree", "v")
            em.setmark("bias_v", "v")

        def attn_steps(c, phj=None):
            nkv = 4 * (c + 1)
            steps = []

            def st_exp(pr, i, kt):
                qoff = 0 if i < 4 * c else (i - 4 * c) * 128
                slot = (i % 2) * 2
                def f():
                    em.op("tensor", lambda e: e.matmul(
                        pA[:, qoff:512], B["KT"][0:64, kt, i * 128:(i + 1) * 128],
                        B["QTc"][0:64, c % 2, pr, qoff:512], start=True, stop=True,
                        skip_group_check=True),
                        waits=[em.m("pA_free"), em.m("pB_free"),
                               ("s", (phj or ph1)["s"]),
                               ("dqr", em.cnt["dqr"]), em.m("bias_v")])
                    em.op("tensor", lambda e: e.matmul(
                        pB[:, qoff:512], B["KT"][64:128, kt, i * 128:(i + 1) * 128],
                        B["QTc"][64:128, c % 2, pr, qoff:512], start=True, stop=True,
                        skip_group_check=True), inc=("t", 1))
                    em.setmark(f"st_t{i % 2}", "t")
                    em.op("scalar", lambda e: e.activation(
                        B["PT"][:, slot, qoff:512], pA[:, qoff:512], AF.Exp,
                        bias=B["bias_sb"][:, 0:1], scale=0.125),
                        waits=[em.m(f"st_t{i % 2}"), em.m(f"pt{slot}_free")],
                        inc=("s", 1))
                    em.setmark("pA_free", "s")
                    em.op("scalar", lambda e: e.activation(
                        B["PT"][:, slot + 1, qoff:512], pB[:, qoff:512], AF.Exp,
                        bias=B["bias_sb"][:, 0:1], scale=0.125),
                        waits=[em.m(f"pt{slot + 1}_free")], inc=("s", 1))
                    em.setmark("pB_free", "s")
                    em.setmark(f"exp_s{i % 2}", "s")
                    if i >= 4 * c:
                        em.op("vector", lambda e: e.tensor_tensor(
                            B["PT"][:, slot, qoff:qoff + 128],
                            B["PT"][:, slot, qoff:qoff + 128],
                            B["trim_s"][:], ALU.mult),
                            waits=[em.m(f"exp_s{i % 2}")], inc=("v", 1), lp=True)
                        em.op("vector", lambda e: e.tensor_tensor(
                            B["PT"][:, slot + 1, qoff:qoff + 128],
                            B["PT"][:, slot + 1, qoff:qoff + 128],
                            B["trim_s"][:], ALU.mult), inc=("v", 1), lp=True)
                        em.setmark(f"pm_v{i % 2}", "v")
                    else:
                        em.mark[f"pm_v{i % 2}"] = ("v", 0)
                return f

            def av(pr, i, kt):
                qoff = 0 if i < 4 * c else (i - 4 * c) * 128
                slot = (i % 2) * 2
                def f():
                    avw = [em.m(f"exp_s{i % 2}"), em.m(f"pm_v{i % 2}")]
                    if i == 0:
                        avw += [em.m("avt_v"), em.m("avt_s"),
                                ("s", (phj or ph1)["s"])]
                    em.op("tensor", lambda e: e.matmul(
                        pC[0:65, qoff:512], Vb[:, i, 2 * kt, 0:65],
                        B["PT"][:, slot, qoff:512],
                        start=(i == 0), stop=(i == nkv - 1), skip_group_check=True),
                        waits=avw)
                    em.op("tensor", lambda e: e.matmul(
                        pD[0:65, qoff:512], Vb[:, i, 2 * kt + 1, 0:65],
                        B["PT"][:, slot + 1, qoff:512],
                        start=(i == 0), stop=(i == nkv - 1), skip_group_check=True),
                        inc=("t", 1))
                    em.setmark(f"pt{slot}_free", "t")
                    em.setmark(f"pt{slot + 1}_free", "t")
                return f

            def tail_a(pr, last):
                def f():
                    em.setmark("av_t", "t")
                    em.op("scalar", lambda e: e.activation(
                        B["lnr"][0:1, 0:512], pC[64:65, 0:512], AF.Ln),
                        waits=[em.m("av_t"), em.m("rsb_free")], inc=("s", 1))
                    em.op("scalar", lambda e: e.activation(
                        B["rsb"][0:1, 0:512], B["lnr"][0:1, 0:512], AF.Exp, scale=-1.0),
                        waits=[("s", em.cnt["s"])], inc=("s", 1), lp=True)
                    em.op("scalar", lambda e: e.activation(
                        B["lnr"][0:1, 0:512], pD[64:65, 0:512], AF.Ln),
                        waits=[("s", em.cnt["s"])], inc=("s", 1))
                    em.op("scalar", lambda e: e.activation(
                        B["rsb"][0:1, 512:1024], B["lnr"][0:1, 0:512], AF.Exp,
                        scale=-1.0), waits=[("s", em.cnt["s"])], inc=("s", 1), lp=True)
                    em.setmark("rsb_s", "s")
                    em.op("vector", lambda e: e.tensor_copy(
                        B["attnraw"][0:64, :], pC[0:64, :]),
                        waits=[em.m("av_t"), em.m("araw_free")], inc=("v", 1))
                    em.op("vector", lambda e: e.tensor_copy(
                        B["attnraw"][64:128, :], pD[0:64, :]), inc=("v", 1))
                    em.setmark("araw_v", "v")
                    em.setmark("avt_v", "v")
                    em.setmark("avt_s", "s")
                    if last:
                        em.setmark(f"qtc_free{c % 2}", "t")
                        em.setmark("bias_used_s", "s")
                return f

            def tail_b(pr, last):
                def f():
                    em.op("tensor", lambda e: e.matmul(
                        pE[0:64, :], B["onesb"][0:1, 0:64], B["rsb"][0:1, 0:512],
                        start=True, stop=True, skip_group_check=True),
                        waits=[em.m("rsb_s"), em.m("pEfree")])
                    em.op("tensor", lambda e: e.matmul(
                        pE[64:128, :], B["onesb"][0:1, 0:64], B["rsb"][0:1, 512:1024],
                        start=True, stop=True, skip_group_check=True), inc=("t", 1))
                    em.setmark("pE_t", "t")
                    em.op("vector", lambda e: e.tensor_tensor(
                        B["attnT"][:, c % 2, pr, :], B["attnraw"][:], pE[:], ALU.mult),
                        waits=[em.m("araw_v"), em.m("pE_t"),
                               em.m(f"wo_done{c % 2}")], inc=("v", 1), lp=True)
                    em.setmark("pEfree", "v")
                    em.setmark("araw_free", "v")
                    em.setmark("rsb_free", "v")
                    if last:
                        em.setmark("attnT_v", "v")
                return f

            for pr in range(8):
                kt = 0 if pr < 4 else 1
                for i in range(nkv):
                    steps.append((st_exp(pr, i, kt), 1.0))
                    if i > 0:
                        steps.append((av(pr, i - 1, kt), 1.0))
                steps.append((av(pr, nkv - 1, kt), 1.0))
                steps.append((tail_a(pr, pr == 7), 2.0))
                steps.append((tail_b(pr, pr == 7), 1.5))
            return steps

        def wo_ln1_gate(c):
            steps = []
            def mk_wots(ts):
              def f():
                for dh in range(2):
                    pX = pF if dh == 0 else pG
                    fm = "pFfree" if dh == 0 else "pGfree"
                    tm = "pF_t" if dh == 0 else "pG_t"
                    def wogrp(e, ts=ts, dh=dh, pX=pX):
                        for j in range(8):
                            ins = e.matmul(pX[:],
                                           B["attnT"][:, c % 2, j, ts * 128:(ts + 1) * 128],
                                           Wos[:, j, dh * 512:(dh + 1) * 512],
                                           start=(j == 0), stop=(j == 7))
                        return ins
                    em.op("tensor", wogrp,
                          waits=[em.m("attnT_v"), em.m("w_attn"), em.m(fm)]
                          + ([("v", ph1["v"]), ("s", ph1["s"])] if c == 0 else []),
                          inc=("t", 1))
                    em.setmark(tm, "t")
                    if ts == 3 and dh == 1:
                        em.setmark(f"wo_done{c % 2}", "t")
                    em.op("vector", lambda e, ts=ts, dh=dh, pX=pX: e.tensor_tensor(
                        B["XM"][:, ts, dh * 512:(dh + 1) * 512], pX[:],
                        B["XM"][:, ts, dh * 512:(dh + 1) * 512], ALU.add),
                        waits=[em.m(tm), ("dx", em.cnt["dx"])], inc=("v", 1))
                    em.setmark(fm, "v")
              return f
            def mk_ln1(ts):
              def f():
                em.op("vector", lambda e, ts=ts: e.bn_stats(
                    B["stats"][:, 0:6], B["XM"][:, ts, 0:512]), inc=("v", 1))
                em.op("vector", lambda e, ts=ts: e.bn_stats(
                    B["stats"][:, 6:12], B["XM"][:, ts, 512:1024]), inc=("v", 1))
                em.op("vector", lambda e: e.bn_aggr(
                    B["stats"][:, 12:14], B["stats"][:, 0:12]),
                    waits=[("v", em.cnt["v"])], inc=("v", 1))
                em.op("vector", lambda e: e.tensor_scalar(
                    B["stats"][:, 14:15], B["stats"][:, 13:14], EPS, None, ALU.add),
                    waits=[("v", em.cnt["v"])], inc=("v", 1))
                em.setmark("lnv_v", "v")
                em.op("scalar", lambda e: e.activation(
                    B["stats"][:, 15:16], B["stats"][:, 14:15], AF.Ln),
                    waits=[em.m("lnv_v")], inc=("s", 1))
                em.op("scalar", lambda e: e.activation(
                    B["stats"][:, 14:15], B["stats"][:, 15:16], AF.Exp, scale=-0.5),
                    waits=[("s", em.cnt["s"])], inc=("s", 1))
                em.setmark("rstd_s", "s")
                em.op("vector", lambda e, ts=ts: e.tensor_scalar(
                    B["XM"][:, ts, :], B["XM"][:, ts, :], B["stats"][:, 12:13],
                    B["stats"][:, 14:15], ALU.subtract, ALU.mult),
                    waits=[em.m("rstd_s")], inc=("v", 1))
                em.setmark("x1ap_g", "v")
                em.op("scalar", lambda e, ts=ts: e.copy(
                    B["bst"][:], B["XM"][:, ts, :]),
                    waits=[em.m("bst_tfree"), em.m("x1ap_g")], inc=("s", 1))
                em.setmark("bst_v", "s")
              return f
            def mk_ln1tr(ts):
              def f():
                def x1tr(e):
                    for j in range(8):
                        ins = e.transpose(pH[:, j * 128:(j + 1) * 128],
                                          B["bst"][:, j * 128:(j + 1) * 128],
                                          B["idbf_s"][:])
                    return ins
                em.op("tensor", x1tr, waits=[em.m("bst_v"), em.m("pHfree")],
                      inc=("t", 1))
                em.setmark("pH_t", "t")
                em.setmark("bst_tfree", "t")
                em.op("vector", lambda e, ts=ts: e.tensor_copy(
                    B["X1T"][:, 0:8, ts * 128:(ts + 1) * 128],
                    pH[:].rearrange("p (j q) -> p j q", j=8)),
                    waits=[em.m("pH_t"), em.m("x1t_free")], inc=("v", 1))
                em.setmark("pHfree", "v")
                if ts == 3:
                    em.setmark("x1t_v", "v")
              return f
            def mk_gate(ts):
              def f():
                def ggrp(e, ts=ts):
                    for j in range(8):
                        e.matmul(pE[:, 0:E], B["X1T"][:, j, ts * 128:(ts + 1) * 128],
                                 Wgs[:, j, :], start=(j == 0), stop=False)
                    return e.matmul(pE[:, 0:E], B["onesb"][0:1, 0:128],
                                    B["bg_s"][0:1, :], start=False, stop=True)
                em.op("tensor", ggrp,
                      waits=[em.m("x1t_v"), em.m("pEfree"), em.m("w_attn")],
                      inc=("t", 1))
                em.setmark("pE_t", "t")
                g = B["gsc"]
                em.op("vector", lambda e: e.tensor_copy(g[:, 0:8], pE[:, 0:E]),
                      waits=[em.m("pE_t")], inc=("v", 1))
                em.setmark("pEfree", "v")
                em.op("vector", lambda e: e.reduce_max(g[:, 8:9], g[:, 0:8], axis=AX.X),
                      waits=[("v", em.cnt["v"])], inc=("v", 1))
                em.op("vector", lambda e: e.tensor_scalar(
                    g[:, 16:24], g[:, 0:8], g[:, 8:9], -1e30, ALU.is_ge, ALU.mult),
                    waits=[("v", em.cnt["v"])], inc=("v", 1))
                em.op("vector", lambda e: e.tensor_tensor(
                    g[:, 16:24], g[:, 16:24], g[:, 0:8], ALU.add),
                    waits=[("v", em.cnt["v"])], inc=("v", 1))
                em.op("vector", lambda e: e.reduce_max(g[:, 9:10], g[:, 16:24], axis=AX.X),
                      waits=[("v", em.cnt["v"])], inc=("v", 1))
                em.op("vector", lambda e: e.tensor_scalar(
                    g[:, 16:24], g[:, 0:8], g[:, 9:10], None, ALU.is_ge),
                    waits=[("v", em.cnt["v"])], inc=("v", 1))
                em.op("vector", lambda e, ts=ts: e.tensor_tensor(
                    B["wch"][:, ts, :], g[:, 16:24], g[:, 0:8], ALU.mult),
                    waits=[("v", em.cnt["v"])], inc=("v", 1))
                em.setmark("wch_v", "v")
              return f
            def mk_gatetr(ts):
              def f():
                em.op("tensor", lambda e, ts=ts: e.transpose(
                    pE[0:8, 0:128], B["wch"][:, ts, :], B["idf_s"][:]),
                    waits=[em.m("wch_v"), em.m("pEfree")], inc=("t", 1))
                em.setmark("pE_t", "t")
                em.op("vector", lambda e, ts=ts: e.tensor_copy(
                    B["wTb"][0:8, ts, :], pE[0:8, 0:128]),
                    waits=[em.m("pE_t"), ("dr", 16 * st["nrow"])], inc=("v", 1))
                em.setmark("pEfree", "v")
                if ts == 3:
                    em.setmark("wtb_v", "v")
              return f
            for ts in range(4):
                steps.append((mk_wots(ts), 4.0))
                steps.append((mk_ln1(ts), 4.0))
                steps.append((mk_ln1tr(ts), 2.0))
            for ts in range(4):
                steps.append((mk_gate(ts), 1.5))
                steps.append((mk_gatetr(ts), 1.0))
            return steps

        def ffn_steps(c):
            steps = []
            for dt in range(8):
                def b2init(dt=dt):
                    pX = pF if dt % 2 == 0 else pG
                    fm = "pFfree" if dt % 2 == 0 else "pGfree"
                    tm = "pF_t" if dt % 2 == 0 else "pG_t"
                    em.op("tensor", lambda e: e.matmul(
                        pX[:], B["b2_s"][0:8, dt * 128:(dt + 1) * 128],
                        B["wTb"][:].rearrange("p a b -> p (a b)"), start=True, stop=True),
                        waits=[em.m("wtb_v"), em.m(fm), em.m("w_attn")], inc=("t", 1))
                    em.setmark(tm, "t")
                    em.op("vector", lambda e: e.tensor_copy(
                        B["Yacc"][:, dt, :], pX[:]), waits=[em.m(tm)], inc=("v", 1))
                    em.setmark(fm, "v")
                steps.append((b2init, 1.0))
            for e_i in range(E):
                def loads_wb(e_i=e_i):
                    sl1 = st["wseq"] % 3
                    ph1g = ([("t", ph1["t"]), ("v", ph1["v"]), ("s", ph1["s"])]
                            if (c == 0 and e_i == 0) else [])
                    em.op("gpsimd", lambda e: e.dma_start(
                        out=B["Wsl"][:, sl1, :, :],
                        in_=dr["w1"][e_i].rearrange("(j p) h -> p j h", p=128)),
                        waits=[em.m(f"wsl{sl1}_free")] + ph1g, inc=("dw", 16))
                    em.mark[f"w1ld.{c}.{e_i}"] = ("dw", em.cnt["dw"])
                    em.mark[f"w1sl.{c}.{e_i}"] = sl1
                    st["wseq"] += 1
                    sl2 = st["wseq"] % 3
                    em.op("gpsimd", lambda e: e.dma_start(
                        out=B["Wsl"][:, sl2, :, :],
                        in_=dr["w2"][e_i].rearrange("(j p) h -> p j h", p=128)),
                        waits=[em.m(f"wsl{sl2}_free")], inc=("dw", 16))
                    em.mark[f"w2ld.{c}.{e_i}"] = ("dw", em.cnt["dw"])
                    em.mark[f"w2sl.{c}.{e_i}"] = sl2
                    st["wseq"] += 1
                    em.op("sync", lambda e: e.dma_start(
                        out=B["wrow"][0:1, e_i % 2, :],
                        in_=B["wTb"][e_i:e_i + 1, :, :].rearrange("p a b -> p (a b)")),
                        waits=[em.m("wtb_v"), em.m(f"wrow{e_i % 2}_free")],
                        inc=("dr", 16))
                    st["nrow"] += 1
                    em.op("tensor", lambda e: e.matmul(
                        pE[:], B["onesb"][0:1, 0:128], B["wrow"][0:1, e_i % 2, :],
                        start=True, stop=True, skip_group_check=True),
                        waits=[("dr", 16 * st["nrow"]), em.m("pEfree")], inc=("t", 1))
                    em.setmark("pE_t", "t")
                    em.setmark(f"wrow{e_i % 2}_free", "t")
                    em.op("vector", lambda e: e.tensor_copy(B["wb_sb"][:], pE[:]),
                          waits=[em.m("pE_t"), em.m("wb_free")], inc=("v", 1))
                    em.setmark("pEfree", "v")
                    em.setmark("wb_v", "v")
                steps.append((loads_wb, 1.0))
                for ht in range(8):
                    def hstep(e_i=e_i, ht=ht):
                        sl1 = em.mark[f"w1sl.{c}.{e_i}"]
                        pX = pF if ht % 2 == 0 else pG
                        fm = "pFfree" if ht % 2 == 0 else "pGfree"
                        tm = "pF_t" if ht % 2 == 0 else "pG_t"
                        def hgrp(e):
                            for j in range(8):
                                ins = e.matmul(pX[:],
                                               B["Wsl"][:, sl1, j, ht * 128:(ht + 1) * 128],
                                               B["X1T"][:, j, :], start=(j == 0), stop=(j == 7))
                            return ins
                        em.op("tensor", hgrp,
                              waits=[em.m(f"w1ld.{c}.{e_i}"), em.m("x1t_v"), em.m(fm)],
                              inc=("t", 1))
                        em.setmark(tm, "t")
                        if ht == 7:
                            em.setmark(f"wsl{sl1}_free", "t")
                        em.op("vector", lambda e: e.tensor_scalar(
                            B["Hb"][:, ht % 2, :], pX[:],
                            B["b1r_s"][:, e_i * 8 + ht:e_i * 8 + ht + 1], 0.0,
                            ALU.add, ALU.max),
                            waits=[em.m(tm), ("dc", 16 * 5), em.m(f"hb{ht % 2}_free")],
                            inc=("v", 1), lp=True)
                        em.setmark(fm, "v")
                        em.setmark("hb_s", "v")
                        em.op("vector", lambda e: e.tensor_tensor(
                            B["Hw"][:, ht, :], B["Hb"][:, ht % 2, :], B["wb_sb"][:],
                            ALU.mult),
                            waits=[em.m("hb_s"), em.m("wb_v"), em.m("hw_free")],
                            inc=("v", 1), lp=True)
                        em.setmark(f"hb{ht % 2}_free", "v")
                        if ht == 7:
                            em.setmark("hw_v", "v")
                            em.setmark("wb_free", "v")
                    steps.append((hstep, 2.0))
                for dt in range(8):
                    def ystep(e_i=e_i, dt=dt):
                        sl2 = em.mark[f"w2sl.{c}.{e_i}"]
                        pX = pF if dt % 2 == 0 else pG
                        fm = "pFfree" if dt % 2 == 0 else "pGfree"
                        tm = "pF_t" if dt % 2 == 0 else "pG_t"
                        def ygrp(e):
                            for ht in range(8):
                                ins = e.matmul(pX[:],
                                               B["Wsl"][:, sl2, ht, dt * 128:(dt + 1) * 128],
                                               B["Hw"][:, ht, :], start=(ht == 0), stop=(ht == 7))
                            return ins
                        em.op("tensor", ygrp,
                              waits=[em.m(f"w2ld.{c}.{e_i}"), em.m("hw_v"), em.m(fm)],
                              inc=("t", 1))
                        em.setmark(tm, "t")
                        if dt == 7:
                            em.setmark(f"wsl{sl2}_free", "t")
                            em.setmark("hw_free", "t")
                        em.op("vector", lambda e: e.tensor_tensor(
                            B["Yacc"][:, dt, :], pX[:], B["Yacc"][:, dt, :], ALU.add),
                            waits=[em.m(tm)], inc=("v", 1))
                        em.setmark(fm, "v")
                    steps.append((ystep, 2.0))
            return steps

        def ln2_out(c):
            steps = []
            def mk_ln2a(ts):
              def f():
                em.op("scalar", lambda e, ts=ts: e.copy(
                    B["bst"][:].rearrange("p (a b) -> p a b", a=8),
                    B["Yacc"][:, :, ts * 128:(ts + 1) * 128]),
                    waits=[em.m("bst_tfree"), ("v", em.cnt["v"])], inc=("s", 1))
                em.setmark("ybf_v", "s")
              return f
            def mk_ln2(ts):
              def f():
                zs = ts % 2
                def ytr(e):
                    for dt in range(8):
                        ins = e.transpose(pH[:, dt * 128:(dt + 1) * 128],
                                          B["bst"][:, dt * 128:(dt + 1) * 128],
                                          B["idbf_s"][:])
                    return ins
                em.op("tensor", ytr, waits=[em.m("ybf_v"), em.m("pHfree")],
                      inc=("t", 1))
                em.setmark("pH_t", "t")
                em.setmark("bst_tfree", "t")
                em.op("vector", lambda e, ts=ts, zs=zs: e.tensor_tensor(
                    B["z2"][:, zs, :], pH[:], B["XM"][:, ts, :], ALU.add),
                    waits=[em.m("pH_t"), ("do", 16 * st["z2war"][zs])], inc=("v", 1))
                em.setmark("pHfree", "v")
                em.op("vector", lambda e, zs=zs: e.bn_stats(
                    B["stats"][:, 0:6], B["z2"][:, zs, 0:512]), inc=("v", 1))
                em.op("vector", lambda e, zs=zs: e.bn_stats(
                    B["stats"][:, 6:12], B["z2"][:, zs, 512:1024]), inc=("v", 1))
                em.op("vector", lambda e: e.bn_aggr(
                    B["stats"][:, 12:14], B["stats"][:, 0:12]),
                    waits=[("v", em.cnt["v"])], inc=("v", 1))
                em.op("vector", lambda e: e.tensor_scalar(
                    B["stats"][:, 14:15], B["stats"][:, 13:14], EPS, None, ALU.add),
                    waits=[("v", em.cnt["v"])], inc=("v", 1))
                em.setmark("lnv_v", "v")
                em.op("scalar", lambda e: e.activation(
                    B["stats"][:, 15:16], B["stats"][:, 14:15], AF.Ln),
                    waits=[em.m("lnv_v")], inc=("s", 1))
                em.op("scalar", lambda e: e.activation(
                    B["stats"][:, 14:15], B["stats"][:, 15:16], AF.Exp, scale=-0.5),
                    waits=[("s", em.cnt["s"])], inc=("s", 1))
                em.setmark("rstd_s", "s")
                em.op("vector", lambda e, zs=zs: e.tensor_scalar(
                    B["z2"][:, zs, :], B["z2"][:, zs, :], B["stats"][:, 12:13],
                    B["stats"][:, 14:15], ALU.subtract, ALU.mult),
                    waits=[em.m("rstd_s")], inc=("v", 1))
                em.setmark("x2_v", "v")
                em.op("sync", lambda e, c=c, ts=ts, zs=zs: e.dma_start(
                    out=out[c * CH + ts * 128: c * CH + (ts + 1) * 128, :],
                    in_=B["z2"][:, zs, :]),
                    waits=[em.m("x2_v")], inc=("do", 16))
                st["nout"] += 1
                st["z2war"][zs] = st["nout"]
                if ts == 3:
                    em.setmark("xm_free", "v")
                    em.setmark("x1t_free", "t")
              return f
            for ts in range(4):
                steps.append((mk_ln2a(ts), 1.5))
                steps.append((mk_ln2(ts), 5.0))
            return steps

        def weave(A, F):
            wa = sum(w for _, w in A) or 1e-9
            wf = sum(w for _, w in F) or 1e-9
            ia = ifl = 0
            ca = cf = 0.0
            while ia < len(A) or ifl < len(F):
                if ifl < len(F) and (ia >= len(A) or cf / wf <= ca / wa):
                    F[ifl][0]()
                    cf += F[ifl][1]
                    ifl += 1
                else:
                    A[ia][0]()
                    ca += A[ia][1]
                    ia += 1

        rest = []
        for c_ in range(NCH_):
            rest += ph1_steps(c_)
        weave([], rest)
        ph1.update(em.cnt)
        ph1c0.update(em.cnt)
        attn_prologue(0, ph1c0)
        weave(attn_steps(0, ph1c0), [])
        dump("d_xt", B["XTc"][:, (S // CH - 1) % 2, :, :])
        dump("d_qst", B["QTstage"][:])
        dump("d_kt", KT[:])
        dump("d_vb", Vb[:])
        dump("d_km", B["kmax2"][:])
        dump("d_qm", B["qmax2"][:])
        em.op("sync", lambda e: e.dma_start(
            out=B["XM"][:],
            in_=dr["x"][0:CH, :].rearrange("(ts p) d -> p ts d", p=128)),
            waits=[("t", ph1["t"]), ("v", ph1["v"]), ("s", ph1["s"]),
                   ("dx", em.cnt["dx"])], inc=("dx", 16))
        for c in range(NCH_):
            def xm_load(c=c):
                em.op("sync", lambda e: e.dma_start(
                    out=B["XM"][:],
                    in_=dr["x"][(c + 1) * CH:(c + 2) * CH, :].rearrange(
                        "(ts p) d -> p ts d", p=128)),
                    waits=[em.m("xm_free")], inc=("dx", 16))
            F = wo_ln1_gate(c)
            if c + 1 < NCH_:
                attn_prologue(c + 1)
                F = F + ffn_steps(c) + ln2_out(c) + [(xm_load, 0.5)]
                weave(attn_steps(c + 1), F)
            else:
                F = F + ffn_steps(c) + ln2_out(c)
                weave([], F)

        dump("d_qtc", B["QTc"][:, (NCH_ - 1) % 2, :, :])
        dump("d_attnt", B["attnT"][:, (NCH_ - 1) % 2, :, :])
        dump("d_x1", B["XM"][:])
        dump("d_x1t", B["X1T"][:])
        dump("d_w", B["wch"][:])
        dump("d_yacc", B["Yacc"][:])
        dump("d_bias", B["bias_sb"][:])
        dump("d_pt", B["PT"][:])

        em.op("sync", lambda e: e.wait_ge(sems["do"], 16 * st["nout"]))

    with ctx:
        with nc.Block() as block:
            for name, attr in (("sync", block.sync), ("gpsimd", block.gpsimd),
                               ("tensor", block.tensor), ("vector", block.vector),
                               ("scalar", block.scalar)):
                def body(eng, name=name):
                    sched(Em(nc, sems, name, eng, serialize=serialize))
                attr(body)
    return nc


def host_inputs(x_core, Wq, Wk, Wv, Wo, Wg, bg, W1, b1, W2, b2, S):
    wq_p = Wq.reshape(DM, NH, DH)[:, PERM, :].reshape(DM, DM)
    wo_p = Wo.reshape(NH, DH, DM)[PERM].reshape(DM, DM)
    b1r = b1.reshape(E, 8, 128).transpose(2, 0, 1).reshape(128, 64)
    C, Sg = rope_tables(S)
    idn = np.eye(128, dtype=np.float32)
    tri = (np.arange(128)[:, None] <= np.arange(128)[None, :]).astype(np.float32)
    psw = np.zeros((128, 128), np.float32)
    for k in range(128):
        psw[k, k ^ 1] = 1.0
    return {
        "x": np.ascontiguousarray(x_core, dtype=np.float32),
        "wq": np.ascontiguousarray(wq_p, dtype=np.float32),
        "wk": np.ascontiguousarray(Wk, dtype=np.float32),
        "wv": np.ascontiguousarray(Wv, dtype=np.float32),
        "wo": np.ascontiguousarray(wo_p, dtype=np.float32),
        "wg": np.ascontiguousarray(Wg, dtype=np.float32),
        "bg": np.ascontiguousarray(bg.reshape(1, E), dtype=np.float32),
        "w1": np.ascontiguousarray(W1, dtype=np.float32),
        "b1r": np.ascontiguousarray(b1r, dtype=np.float32),
        "w2": np.ascontiguousarray(W2, dtype=np.float32),
        "b2": np.ascontiguousarray(b2, dtype=np.float32),
        "ropec": C,
        "ropes": Sg,
        "idbf": idn.astype(BF),
        "idf": idn,
        "trim": tri.astype(BF),
        "pswap": psw.astype(BF),
    }


def kernel(L0_attn_hidden_states, L0_attn_Wq, L0_attn_Wk, L0_attn_Wv, L0_attn_Wo,
           L0_ffn_W_gate, L0_ffn_b_gate, L0_ffn_W1, L0_ffn_b1, L0_ffn_W2, L0_ffn_b2):
    x = np.asarray(L0_attn_hidden_states, dtype=np.float32)
    Bn, S, _ = x.shape
    args = [np.asarray(a, dtype=np.float32) for a in
            (L0_attn_Wq, L0_attn_Wk, L0_attn_Wv, L0_attn_Wo, L0_ffn_W_gate,
             L0_ffn_b_gate, L0_ffn_W1, L0_ffn_b1, L0_ffn_W2, L0_ffn_b2)]
    nc = build_nc(S)
    in_maps = [host_inputs(x[b], *args, S) for b in range(Bn)]
    res = run_bass_kernel_spmd(nc, in_maps, core_ids=list(range(Bn)))
    return np.stack([np.asarray(r["out"]) for r in res.results], axis=0)
